# version 18
# baseline (speedup 1.0000x reference)
import numpy as np
import concourse.bass as bass
import concourse.mybir as mybir
from concourse.bass_utils import run_bass_kernel_spmd
from concourse.alu_op_type import AluOpType as OP

F32 = mybir.dt.float32
I32 = mybir.dt.int32
AF = mybir.ActivationFunctionType

T = 4096
D = 1024
NSUP = 8
SUP = 512
NSUB = 4
NK = D // 128
EH = 2048
ROPE_BASE = 500000.0
TWO_PI = 2.0 * np.pi
CW1 = float(np.float32(6.28125))
CW2 = float(np.float32(TWO_PI - 6.28125))
PI = float(np.pi)
LNB32 = float(-np.log(ROPE_BASE) / 32.0)
LNEPS = 1e-5
BIGNEG = -2.0


class Prog:
    ENGS = ("sync", "tensor", "vector", "scalar", "gpsimd")

    def __init__(self, nc, sems):
        self.nc = nc
        self.sems = sems
        self.plan = {e: [] for e in self.ENGS}
        self.emitted = {s: 0 for s in sems}
        self.hwm = {(e, s): 0 for e in self.ENGS for s in sems}

    def emit(self, eng, fn, waits=(), inc=None, inc_n=1):
        w = {}
        for tok in waits:
            if tok is None:
                continue
            s, v, peng = tok
            if peng == eng:
                continue
            assert v <= self.emitted[s], f"wait on future {s}:{v}>{self.emitted[s]}"
            if v > self.hwm[(eng, s)] and v > w.get(s, 0):
                w[s] = v
        for s, v in w.items():
            self.hwm[(eng, s)] = v
        tok = None
        if inc is not None:
            self.emitted[inc] += inc_n
            tok = (inc, self.emitted[inc], None if inc_n == 16 else eng)
        self.plan[eng].append((tuple(w.items()), fn, inc, inc_n))
        return tok

    def dma(self, eng, out, in_, waits=(), sem="dma", slow=False):
        if slow:
            return self.emit(eng, lambda e, o=out, i=in_: e.dma_start(out=o, in_=i,
                             allow_slow_non_contiguous=True), waits=waits, inc=sem, inc_n=16)
        return self.emit(eng, lambda e, o=out, i=in_: e.dma_start(out=o, in_=i),
                         waits=waits, inc=sem, inc_n=16)

    def simulate(self):
        import collections
        heads = {e: 0 for e in self.ENGS}
        sems = collections.defaultdict(int)
        progress = True
        while progress:
            progress = False
            for e in self.ENGS:
                while heads[e] < len(self.plan[e]):
                    waits, fn, inc, inc_n = self.plan[e][heads[e]]
                    if all(sems[s] >= v for s, v in waits):
                        if inc is not None:
                            sems[inc] += inc_n
                        heads[e] += 1
                        progress = True
                    else:
                        break
        stuck = {e: heads[e] for e in self.ENGS if heads[e] < len(self.plan[e])}
        if stuck:
            print("DEADLOCK detected; blocked heads:")
            for e, h in stuck.items():
                waits, fn, inc, inc_n = self.plan[e][h]
                print(f"  {e}[{h}/{len(self.plan[e])}] waits {waits} (sems: "
                      + ", ".join(f"{s}={sems[s]}" for s, v in waits) + ")")
            raise RuntimeError("deadlock")

    def replay(self, block):
        for ename in self.ENGS:
            section = getattr(block, ename)

            def make(ename):
                def _f(eng):
                    for waits, fn, inc, inc_n in self.plan[ename]:
                        for (s, v) in waits:
                            eng.wait_ge(self.sems[s], v)
                        if fn is None:
                            continue
                        ins = fn(eng)
                        if inc is not None:
                            if ename == "gpsimd" and inc_n != 16:
                                ins = eng.drain()
                            ins.then_inc(self.sems[inc], inc_n)
                return _f
            section(make(ename))


def build_core(debug=(), phases=2, nsup=NSUP, dbg_tile=None):
    nc = bass.Bass("TRN2", target_bir_lowering=False)
    dbg_t = {}

    xT_d = nc.dram_tensor("xT", [D, T], F32, kind="ExternalInput").ap()
    wqkv_d = nc.dram_tensor("wqkv", [D, 3 * D], F32, kind="ExternalInput").ap()
    wout_d = nc.dram_tensor("wout", [D, D], F32, kind="ExternalInput").ap()
    encw_d = nc.dram_tensor("encw", [D, EH], F32, kind="ExternalInput").ap()
    encb_d = nc.dram_tensor("encb", [EH], F32, kind="ExternalInput").ap()
    s2cw_d = nc.dram_tensor("s2cw", [EH, 64], F32, kind="ExternalInput").ap()
    s2cb_d = nc.dram_tensor("s2cb", [64], F32, kind="ExternalInput").ap()
    rw1_d = nc.dram_tensor("rw1", [64, 64], F32, kind="ExternalInput").ap()
    rb1_d = nc.dram_tensor("rb1", [64], F32, kind="ExternalInput").ap()
    rw2_d = nc.dram_tensor("rw2", [64, 8], F32, kind="ExternalInput").ap()
    rb2_d = nc.dram_tensor("rb2", [8], F32, kind="ExternalInput").ap()
    we1_d = nc.dram_tensor("we1", [8, 64, D], F32, kind="ExternalInput").ap()
    be1_d = nc.dram_tensor("be1", [8, D], F32, kind="ExternalInput").ap()
    we2_d = nc.dram_tensor("we2", [8, D, 64], F32, kind="ExternalInput").ap()
    be2_d = nc.dram_tensor("be2", [8, 64], F32, kind="ExternalInput").ap()
    c2sw_d = nc.dram_tensor("c2sw", [64, EH], F32, kind="ExternalInput").ap()
    c2sb_d = nc.dram_tensor("c2sb", [EH], F32, kind="ExternalInput").ap()
    decw_d = nc.dram_tensor("decw", [EH, D], F32, kind="ExternalInput").ap()
    decb_d = nc.dram_tensor("decb", [D], F32, kind="ExternalInput").ap()
    lng_d = nc.dram_tensor("lng", [D], F32, kind="ExternalInput").ap()
    lnb_d = nc.dram_tensor("lnb", [D], F32, kind="ExternalInput").ap()

    BF16 = mybir.dt.bfloat16
    ln1h_d = nc.dram_tensor("ln1h", [D, T], BF16, kind="Internal").ap()
    ln1l_d = nc.dram_tensor("ln1l", [D, T], BF16, kind="Internal").ap()
    wqh_d = nc.dram_tensor("wqh", [D, D], BF16, kind="Internal").ap()
    wql_d = nc.dram_tensor("wql", [D, D], BF16, kind="Internal").ap()
    wouth_d = nc.dram_tensor("wouth", [D, D], BF16, kind="Internal").ap()
    woutl_d = nc.dram_tensor("woutl", [D, D], BF16, kind="Internal").ap()
    ench_d = nc.dram_tensor("ench", [D, EH], BF16, kind="Internal").ap()
    encl_d = nc.dram_tensor("encl", [D, EH], BF16, kind="Internal").ap()
    outT_d = nc.dram_tensor("outT", [D, T], F32, kind="ExternalOutput").ap()

    def mkdbg(name, shape):
        if name in debug:
            dbg_t[name] = nc.dram_tensor("dbg_" + name, shape, F32, kind="ExternalOutput").ap()
            return dbg_t[name]
        return None

    dbg_ln1 = mkdbg("ln1T", [D, T])
    dbg_kv = mkdbg("kv_sb", [128, 1040])
    dbg_kf = mkdbg("kfe0", [128, 1024])
    dbg_vx = mkdbg("vext0", [128, 1040])
    dbg_kraw = mkdbg("kraw0", [128, 1024])
    dbg_bd = mkdbg("bd", [128, 128])
    dbg_qf = mkdbg("qfT", [D, T])
    dbg_wq = mkdbg("wqbuf", [128, 1024])
    dbg_whd = mkdbg("wqhd", [128, 512])
    dbg_qraw = mkdbg("qraw", [128, 1024])
    dbg_lnb = mkdbg("lnbuf", [128, 1024])
    dbg_den = mkdbg("invden", [16, T])
    dbg_att = mkdbg("attT", [D, T])
    dbg_y = mkdbg("yT", [D, T])
    dbg_ln2 = mkdbg("ln2T", [D, T])
    dbg_cont = mkdbg("contT", [64, T])
    dbg_spk = mkdbg("spk0", [128, 4096])
    dbg_encp = mkdbg("encp", [128, 512])
    dbg_lgtm = mkdbg("lgtm", [128, 32])
    dbg_tanh = mkdbg("tanh0", [64, 512])
    dbg_gd = mkdbg("gdbg", [128, 112])
    dbg_lgf = mkdbg("lgf", [8, 512])
    dbg_gtm2 = mkdbg("gtm2", [128, 32])
    dbg_ln2w = mkdbg("ln2w", [128, 512])
    dbg_encw = mkdbg("encwb", [128, 2048])
    dbg_gat = mkdbg("gatesT", [8, T])
    dbg_acc = mkdbg("accT", [64, T])
    dbg_dec = mkdbg("decT", [D, T])

    from contextlib import ExitStack
    es = ExitStack()

    def SB(name, p, cols, dt=F32):
        return es.enter_context(nc.sbuf_tensor("sb_" + name, [p, cols], dt))

    with es:
        ident = SB("ident", 128, 128)
        onescol = SB("onescol", 128, 1)
        psel16 = SB("psel16", 128, 1024)
        invf_col = SB("invf_col", 128, 1)
        sgn_col = SB("sgn_col", 128, 1)
        invf_rep = SB("invf_rep", 128, 64)
        iotap = SB("iotap", 128, 1)
        postm = SB("postm", 128, 1)
        kv_sb = SB("kv_sb", 128, 1040)
        bd = SB("bd", 128, 128)
        rowp = SB("rowp", 128, 1536)
        C_ST, C_RT, C_ID = 0, 512, 1024
        s2cw = SB("s2cw", 128, 1024)
        rw1 = SB("rw1", 64, 64)
        rw2t = SB("rw2t", 128, 8)
        be1c = SB("be1c", 128, 64)
        be2 = SB("be2", 8, 64)
        encb = SB("encb", 128, 16)
        decb = SB("decb", 128, 8)
        s2cb = SB("s2cb", 64, 1)
        rb1t = SB("rb1t", 128, 1)
        rb2 = SB("rb2", 8, 1)
        c2sb = SB("c2sb", 128, 16)
        lng = SB("lng", 128, 8)
        lnb = SB("lnb", 128, 8)
        cosrep = SB("cosrep", 128, 512)
        sinB = SB("sinB", 128, 512)
        rs1 = SB("rs1", 128, 512)
        rs2 = SB("rs2", 128, 512)
        rs3 = SB("rs3", 128, 512)
        costm = SB("costm", 128, 64)
        sintm = SB("sintm", 128, 64)
        rtm1 = SB("rtm1", 128, 64)
        rtm2 = SB("rtm2", 128, 64)
        th1 = SB("th1", 128, 512)
        th2 = SB("th2", 128, 512)
        pos0 = SB("pos0", 128, 512)
        invft = SB("invft", 128, 512)
        abc = SB("abc", 128, 512)
        bbc = SB("bbc", 128, 512)
        AR1 = SB("AR1", 128, 16384)
        AR2 = SB("AR2", 128, 12288)
        AR3 = SB("AR3", 128, 8192)
        AR5 = SB("AR5", 128, 4096)
        contT = SB("contT", 64, 512)
        accT = SB("accT", 64, 512)
        gtm = SB("gtm", 128, 32)

        pbt = [es.enter_context(nc.psum_tensor(f"pb{i}", [128, 512], F32)) for i in range(8)]
        pb = [t.ap() for t in pbt]

        sems = {}
        for sname in ("dma", "dst", "T", "V", "S", "G",
                      "Lln", "LxT", "Lwq0", "Lwq1", "Lwo0", "Lwo1",
                      "Len0", "Len1", "Ldc0", "Ldc1",
                      "Lw10", "Lw11", "Lw20", "Lw21",
                      "Lc2", "Lsm", "Lsp", "Dsp", "Ds2", "Sln",
                      "LxA", "LlA", "LkA"):
            sems[sname] = es.enter_context(nc.semaphore(sname))
        block = es.enter_context(nc.Block())
        p = Prog(nc, sems)
        p.dbg_theta = None
        p.dbg_theta_tok = None

        wkv = AR1.ap()
        a_xT = AR2[:, 0:4096]
        a_ln1 = AR2[:, 4096:8192]
        k_raw = AR3[:, 0:1024]
        ks1 = AR3[:, 1024:2048]
        ks2 = AR3[:, 2048:3072]
        _kf16 = AR3[:, 3072:4096].bitcast(mybir.dt.bfloat16)
        kfh = _kf16[:, 0:1024]
        kfl = _kf16[:, 1024:2048]
        _vx16 = AR3[:, 4096:5136].bitcast(mybir.dt.bfloat16)
        vxh = _vx16[:, 0:1040]
        vxl = _vx16[:, 1040:2080]
        b_enc = [AR1[:, 0:2048], AR1[:, 2048:4096]]
        b_dec = [AR1[:, 4096:5120], AR1[:, 5120:6144]]
        b_wq = [AR1[:, 6144:7168], AR1[:, 7168:8192]]
        b_wo = [AR1[:, 8192:9216], AR1[:, 9216:10240]]
        b_we1 = [AR1[0:64, 10240:11264], AR1[0:64, 11264:12288]]
        b_we2 = [AR1[:, 12288:12800], AR1[:, 12800:13312]]
        b_c2s = AR1[0:64, 13312:15360]
        b_ln1 = AR2[:, 0:4096]
        b_ln2 = AR2[:, 0:4096]
        _ln16 = AR2[:, 0:4096].bitcast(mybir.dt.bfloat16)
        lnh = _ln16[:, 0:4096]
        lnl = _ln16[:, 4096:8192]
        _a16 = AR2[:, 4096:8192].bitcast(mybir.dt.bfloat16)
        a_ln1h = _a16[:, 0:4096]
        a_ln1l = _a16[:, 4096:8192]
        _att16 = AR3[:, 0:4096].bitcast(mybir.dt.bfloat16)
        atth = _att16[:, 0:4096]
        attl = _att16[:, 4096:8192]
        b_xT = AR2[:, 4096:8192]
        b_he = AR2[:, 4096:8192]
        b_qf = AR2[:, 8192:12288]
        b_decT = AR2[:, 8192:12288]
        b_att = AR3[:, 0:4096]
        b_outT = AR3[:, 0:4096]
        b_yT = AR3[:, 4096:8192]
        b_spk = AR5.ap()
        b_moe = AR5.ap()
        b_moe1 = AR3[:, 0:4096]

        pfree = {i: None for i in range(8)}
        fr = {}

        t0 = p.emit("vector", lambda e: e.memset(ident[:], 1.0), inc="V")
        tok_ident = p.emit("gpsimd", lambda e: e.affine_select(
            ident[:], ident[:], pattern=[[-1, 128]], compare_op=OP.is_equal,
            fill=0.0, base=0, channel_multiplier=1), waits=[t0], inc="G")
        p.emit("vector", lambda e: e.memset(onescol[:], 1.0))
        rowi = rowp[0:1, 0:512]
        t_r = None
        for c in range(128):
            t_r = p.emit("vector", lambda e, c=c: e.memset(rowp[:, c:c + 1], float(c)),
                         inc="V" if c == 127 else None)
        tok_trio = p.emit("tensor", lambda e: e.transpose(pb[0][:, 0:1], rowi[0:1, 0:128],
                                                          ident[0:1, 0:1]),
                          waits=[t_r, tok_ident], inc="T")
        tok_iota = p.emit("vector", lambda e: e.tensor_copy(iotap[:], pb[0][:, 0:1]),
                          waits=[tok_trio], inc="V")
        if dbg_whd is not None:
            tsd = p.emit("vector", lambda e: e.tensor_copy(rs3[:, 0:4], pb[0][:, 0:4]),
                         waits=[tok_trio], inc="V")
            fr["tsd"] = tsd
            p.dma("gpsimd", dbg_whd[:, 0:4], rs3[:, 0:4], waits=[tsd], sem="dst", slow=True)
            p.dma("gpsimd", dbg_whd[0:1, 4:132], rowi[0:1, 0:128], waits=[tok_trio], sem="dst")
            p.dma("gpsimd", dbg_whd[:, 132:133], iotap[:], waits=[tok_iota], sem="dst", slow=True)
            fr["whd_dbg"] = p.dma("gpsimd", dbg_whd[:, 133:261], ident[:], waits=[tok_ident],
                                  sem="dst")
        for v8 in range(8):
            p.emit("vector", lambda e, v8=v8: e.memset(rs1[:, 64 * v8:64 * (v8 + 1)], float(v8)))
            t_r = p.emit("vector", lambda e, v8=v8: e.memset(rs2[:, 64 * v8:64 * (v8 + 1)],
                                                             float(v8 + 8.0)),
                         inc="V" if v8 == 7 else None)
        p.emit("vector", lambda e: e.tensor_scalar(
            psel16[:, 0:512], rs1[:], iotap[:], None, op0=OP.is_equal),
            waits=[tok_iota])
        tok_psel = p.emit("vector", lambda e: e.tensor_scalar(
            psel16[:, 512:1024], rs2[:], iotap[:], None, op0=OP.is_equal), inc="V")
        fr["srow"] = tok_psel
        fr["lnsq0"] = tok_psel
        fr["lnsq1"] = tok_psel
        pfree[0] = tok_iota
        t_r2 = None
        for c in range(128):
            iv = float(np.float32(ROPE_BASE ** (-(c % 32) / 32.0)))
            sv = -1.0 if (c % 64) < 32 else 1.0
            p.emit("vector", lambda e, c=c, iv=iv: e.memset(rowp[:, 128 + c:129 + c], iv))
            t_r2 = p.emit("vector", lambda e, c=c, sv=sv: e.memset(rowp[:, 256 + c:257 + c], sv),
                          inc="V" if c == 127 else None)
        tok_ti2 = p.emit("tensor", lambda e: e.transpose(pb[0][:, 1:2], rowp[0:1, 128:256],
                                                         ident[0:1, 0:1]),
                         waits=[t_r2, tok_iota, fr.get("tsd")], inc="T")
        tok_invc = p.emit("vector", lambda e: e.tensor_copy(invf_col[:], pb[0][:, 1:2]),
                          waits=[tok_ti2], inc="V")
        tok_ti3 = p.emit("tensor", lambda e: e.transpose(pb[0][:, 2:3], rowp[0:1, 256:384],
                                                         ident[0:1, 0:1]), waits=[tok_invc], inc="T")
        tok_sgn = p.emit("vector", lambda e: e.tensor_copy(sgn_col[:], pb[0][:, 2:3]),
                         waits=[tok_ti3], inc="V")
        pfree[0] = tok_sgn
        for c in range(512):
            t_p0 = p.emit("vector", lambda e, c=c: e.memset(pos0[:, c:c + 1], float(c)),
                          inc="V" if c == 511 else None)
        t_o1 = p.emit("vector", lambda e: e.memset(rs3[:], 1.0), inc="V")
        tok_ivm = p.emit("tensor", lambda e: e.matmul(pb[2][:], rowp[0:1, 128:256],
                                                      rs3[0:1, 0:512], start=True, stop=True),
                         waits=[t_o1, t_r2, tok_psel, pfree[2]], inc="T")
        tok_ivt = p.emit("vector", lambda e: e.tensor_copy(invft[:], pb[2][:]),
                         waits=[tok_ivm], inc="V")
        fr["srow"] = tok_ivm
        pfree[2] = tok_ivt
        for j in range(32):
            iv = float(np.float32(ROPE_BASE ** (-j / 32.0)))
            p.emit("vector", lambda e, j=j, iv=iv: e.memset(invf_rep[:, j:j + 1], iv))
            tok_invr = p.emit("vector", lambda e, j=j, iv=iv: e.memset(
                invf_rep[:, 32 + j:33 + j], iv), inc="V" if j == 31 else None)

        if dbg_lnb is not None:
            tcp0 = p.emit("vector", lambda e: e.tensor_copy(costm[:, 0:1], sgn_col[:]),
                          waits=[tok_sgn, tok_invc], inc="V")
            tcp0b = p.emit("vector", lambda e: e.tensor_copy(costm[:, 1:2], invf_col[:]), inc="V")
            fr_dbgA = p.dma("gpsimd", dbg_lnb[:, 0:2], costm[:, 0:2], waits=[tcp0b], sem="dst")
        p.dma("scalar", s2cw.ap().rearrange("p (c m) -> p c m", m=64), s2cw_d.rearrange("(c p) m -> p c m", p=128), sem="Lsm")
        p.dma("scalar", rw1[:], rw1_d, sem="Lsm")
        p.dma("scalar", rw2t[64:128, :], rw2_d, sem="Lsm")
        p.dma("scalar", be1c.ap().rearrange("p (e c) -> p e c", c=8), be1_d.rearrange("e (c p) -> p e c", p=128), slow=True, sem="Lsm")
        p.dma("scalar", be2[:], be2_d, sem="Lsm")
        p.dma("scalar", encb[:], encb_d.rearrange("(c p) -> p c", p=128), slow=True, sem="Lsm")
        p.dma("scalar", decb[:], decb_d.rearrange("(c p) -> p c", p=128), slow=True, sem="Lsm")
        p.dma("scalar", s2cb[:], s2cb_d.rearrange("(c p) -> p c", p=64), slow=True, sem="Lsm")
        p.dma("scalar", rb1t[64:128, :], rb1_d.rearrange("(c p) -> p c", p=64), slow=True, sem="Lsm")
        p.dma("scalar", rb2[:], rb2_d.rearrange("(c p) -> p c", p=8), slow=True, sem="Lsm")
        p.dma("scalar", c2sb[:], c2sb_d.rearrange("(c p) -> p c", p=128), slow=True, sem="Lsm")
        p.dma("scalar", lng[:], lng_d.rearrange("(c p) -> p c", p=128), slow=True, sem="Lsm")
        tok_wsmall = p.dma("scalar", lnb[:], lnb_d.rearrange("(c p) -> p c", p=128), slow=True, sem="Lsm")
        tok_c2sw = p.dma("scalar", b_c2s, c2sw_d, sem="Lc2")
        wkv16 = AR1.ap().bitcast(mybir.dt.bfloat16)
        wkvh = wkv16[:, 0:16384]
        wkvl = wkv16[:, 16384:32768]
        slot_f = [AR5[:, 0:1024], AR5[:, 2048:3072]]
        slot_16 = [AR5[:, 1024:2048].bitcast(mybir.dt.bfloat16),
                   AR5[:, 3072:4096].bitcast(mybir.dt.bfloat16)]
        slot_busy = [None, None]
        slot_i = [0]

        def emit_split(src_ap, hi_dst=None, lo_dst=None, sbuf_hi=None, sbuf_lo=None):
            i = slot_i[0] & 1
            slot_i[0] += 1
            stgf, stg16 = slot_f[i], slot_16[i]
            stgh, stgl = stg16[:, 0:1024], stg16[:, 1024:2048]
            t_ld = p.dma("sync", stgf[:], src_ap, waits=[slot_busy[i]], sem="Lsp")
            p.emit("vector", lambda e: e.tensor_copy(stgh[:], stgf[:]), waits=[t_ld])
            t_lo = p.emit("vector", lambda e: e.tensor_tensor(
                stgl[:], stgf[:], stgh[:], op=OP.subtract), inc="V")
            toks = []
            if hi_dst is not None:
                ds = "Dsp" if i == 0 else "Ds2"
                toks.append(p.dma("gpsimd", hi_dst, stgh[:], waits=[t_lo], sem=ds))
                toks.append(p.dma("gpsimd", lo_dst, stgl[:], waits=[t_lo], sem=ds))
            if sbuf_hi is not None:
                p.emit("vector", lambda e: e.tensor_copy(sbuf_hi, stgh[:]), waits=[t_lo])
                toks.append(p.emit("vector", lambda e: e.tensor_copy(sbuf_lo, stgl[:]), inc="V"))
            slot_busy[i] = toks[-1]
            return toks[-1]

        def ln_fm(chunks, st, b1, b2):
            n = len(chunks)
            sqbuf = [rs1, rs2]
            sq_gate = [fr.get("lnsq0"), fr.get("lnsq1")]
            last_mm = None
            for k, (ch, rdy) in enumerate(chunks):
                sq = sqbuf[k % 2]
                tok_sq = p.emit("scalar", lambda e, c=ch, o=sq: e.activation(o[:], c, AF.Square),
                                waits=[rdy, sq_gate[k % 2], fr.get("rope_dbg")], inc="S")
                p.emit("tensor", lambda e, c=ch, k=k, n=n, st=st: e.matmul(
                    pb[st][0:1, :], onescol[:], c, start=(k == 0), stop=(k == n - 1)),
                    waits=[rdy] + ([pfree[st]] if k == 0 else []))
                last_mm = p.emit("tensor", lambda e, o=sq, k=k, n=n, st=st: e.matmul(
                    pb[st][32:33, :], onescol[:], o[:], start=(k == 0), stop=(k == n - 1)),
                    waits=[tok_sq], inc="T")
                sq_gate[k % 2] = last_mm
            fr["sqmm"] = last_mm
            p.emit("vector", lambda e: e.tensor_copy(rowp[0:1, C_ST:C_ST + 512], pb[st][0:1, :]),
                   waits=[last_mm, fr.get("srow")])
            tok_sr = p.emit("vector", lambda e: e.tensor_copy(rowp[32:33, C_ST:C_ST + 512], pb[st][32:33, :]),
                            inc="V")
            pfree[st] = tok_sr
            tb1 = p.emit("tensor", lambda e, b1=b1: e.matmul(
                pb[b1][:], invft[0:1, 0:128], rowp[0:1, C_ST:C_ST + 512], start=True, stop=True),
                waits=[tok_sr, pfree[b1]], inc="T")
            tb2 = p.emit("tensor", lambda e, b2=b2: e.matmul(
                pb[b2][:], invft[32:33, 0:128], rowp[32:33, C_ST:C_ST + 512], start=True, stop=True),
                waits=[tok_sr, pfree[b2]], inc="T")
            fr["srow"] = tb2
            inv_n = 1.0 / (128 * n)
            t_m = p.emit("vector", lambda e, b1=b1: e.tensor_scalar(rs1[:], pb[b1][:], inv_n, None, op0=OP.mult),
                         waits=[tb1, tok_invc, tok_invr], inc="V")
            t_q = p.emit("vector", lambda e, b2=b2: e.tensor_scalar(rs2[:], pb[b2][:], inv_n, LNEPS,
                                                                     op0=OP.mult, op1=OP.add),
                         waits=[tb2], inc="V")
            fr["lnsq0"], fr["lnsq1"] = t_m, t_q
            pfree[b1] = t_m
            pfree[b2] = t_q
            p.emit("vector", lambda e: e.tensor_tensor(abc[:], rs1[:], rs1[:], op=OP.mult),
                   waits=[fr.get("abcS")])
            p.emit("vector", lambda e: e.tensor_tensor(abc[:], rs2[:], abc[:], op=OP.subtract))
            tok_var = p.emit("vector", lambda e: e.reciprocal(bbc[:], abc[:]), inc="V")
            tok_a = p.emit("scalar", lambda e: e.activation(abc[:], bbc[:], AF.Sqrt),
                           waits=[tok_var], inc="S")
            fr["abcS"] = tok_a
            tok_b = p.emit("vector", lambda e: e.scalar_tensor_tensor(
                bbc[:], rs1[:], -1.0, abc[:], op0=OP.mult, op1=OP.mult), waits=[tok_a], inc="V")
            return tok_a, tok_b

        def rope_sincos(theta, sc_i, sc_f, out_sin, out_cos, first_waits=(), sin_waits=(), cos_waits=()):
            p.emit("vector", lambda e: e.tensor_scalar(sc_f, theta, 1.0 / TWO_PI, None, op0=OP.mult),
                   waits=list(first_waits))
            p.emit("vector", lambda e: e.tensor_copy(sc_i, sc_f))
            p.emit("vector", lambda e: e.tensor_copy(sc_f, sc_i))
            p.emit("vector", lambda e: e.scalar_tensor_tensor(theta, sc_f, -CW1, theta, op0=OP.mult, op1=OP.add))
            p.emit("vector", lambda e: e.scalar_tensor_tensor(theta, sc_f, -CW2, theta, op0=OP.mult, op1=OP.add))
            p.emit("vector", lambda e: e.tensor_scalar(sc_f, theta, -PI, None, op0=OP.is_lt))
            p.emit("vector", lambda e: e.tensor_scalar(out_sin, theta, PI, None, op0=OP.is_gt))
            p.emit("vector", lambda e: e.tensor_tensor(sc_f, sc_f, out_sin, op=OP.subtract))
            tok_th = p.emit("vector", lambda e: e.scalar_tensor_tensor(
                theta, sc_f, TWO_PI, theta, op0=OP.mult, op1=OP.add), inc="V")
            if p.dbg_theta is not None:
                p.dbg_theta_tok = p.dma("gpsimd", p.dbg_theta, theta, waits=[tok_th], sem="dst")
                p.dbg_theta = None
            tok_s = p.emit("scalar", lambda e: e.activation(out_sin, theta, AF.Sin),
                           waits=[tok_th, p.dbg_theta_tok] + list(sin_waits), inc="S")
            tok_abs = p.emit("scalar", lambda e: e.activation(sc_f, theta, AF.Abs), inc="S")
            fr["theta_abs"] = tok_abs
            tok_neg = p.emit("vector", lambda e: e.tensor_scalar(sc_f, sc_f, -1.0, PI / 2,
                                                                 op0=OP.mult, op1=OP.add),
                             waits=[tok_abs], inc="V")
            tok_c = p.emit("scalar", lambda e: e.activation(out_cos, sc_f, AF.Sin),
                           waits=[tok_neg] + list(cos_waits), inc="S")
            return tok_s, tok_c

        KVB = (5, 6, 7)
        xT_free = []
        kbuf_free = None
        first_kv = [True, True, True]
        last_kv_mm = None
        p.emit("vector", lambda e: e.memset(vxh[:], 1.0))
        tok_vext_ones = p.emit("vector", lambda e: e.memset(vxl[:], 0.0), inc="V")

        _a16b = AR2[:, 8192:12288].bitcast(mybir.dt.bfloat16)
        a_ln1X = [(a_ln1h, a_ln1l), (_a16b[:, 0:4096], _a16b[:, 4096:8192])]
        ln1_freeP = {0: [], 1: []}
        ln1_store = {}

        def emit_ln1_stats(s):
            if s == 0:
                xt = xt0_toks
            else:
                xt = []
                for k in range(NK):
                    xt.append(p.dma("sync", a_xT[:, 512 * k:512 * (k + 1)],
                                    xT_d[128 * k:128 * (k + 1), SUP * s:SUP * (s + 1)],
                                    waits=xT_free if k == 0 else (), sem="LxA"))
            chunks = [(a_xT[:, 512 * k:512 * (k + 1)], xt[k]) for k in range(NK)]
            tok_a, tok_b = ln_fm(chunks, 2, 3, 4)
            return xt, tok_a, tok_b

        def emit_ln1_apply(s, xt, tok_a, tok_b):
            nonlocal xT_free
            hT, lT = a_ln1X[s % 2]
            apply_toks = []
            for k in range(NK):
                p.emit("vector", lambda e, k=k: e.tensor_tensor(
                    rs3[:], a_xT[:, 512 * k:512 * (k + 1)], abc[:], op=OP.mult),
                    waits=([tok_a] + ln1_freeP[s % 2]) if k == 0 else [])
                t_ln = p.emit("vector", lambda e, k=k: e.tensor_tensor(rs3[:], rs3[:], bbc[:], op=OP.add),
                              waits=[tok_b] if k == 0 else [], inc="V")
                t_h = p.emit("vector", lambda e, k=k, hT=hT: e.tensor_copy(
                    hT[:, 512 * k:512 * (k + 1)], rs3[:]), waits=[t_ln], inc="V")
                apply_toks.append(p.emit("vector", lambda e, k=k, hT=hT, lT=lT: e.tensor_tensor(
                    lT[:, 512 * k:512 * (k + 1)], rs3[:], hT[:, 512 * k:512 * (k + 1)],
                    op=OP.subtract), waits=[t_h], inc="V"))
            st_toks = []
            for k in range(NK):
                st_toks.append(p.dma("gpsimd", ln1h_d[128 * k:128 * (k + 1), SUP * s:SUP * (s + 1)],
                                     hT[:, 512 * k:512 * (k + 1)], waits=[apply_toks[k]], sem="Sln"))
                st_toks.append(p.dma("gpsimd", ln1l_d[128 * k:128 * (k + 1), SUP * s:SUP * (s + 1)],
                                     lT[:, 512 * k:512 * (k + 1)], waits=[apply_toks[k]], sem="Sln"))
            ln1_store[s] = st_toks[-1]
            xT_free = [apply_toks[-1]]
            return apply_toks, st_toks

        xt0_toks = []
        for k in range(NK):
            xt0_toks.append(p.dma("sync", a_xT[:, 512 * k:512 * (k + 1)],
                                  xT_d[128 * k:128 * (k + 1), 0:SUP], sem="LxA"))
        st0 = emit_ln1_stats(0)
        apply_cur, st_cur = emit_ln1_apply(0, *st0)

        wkv_toks = []
        for k in range(NK):
            emit_split(wqkv_d[128 * k:128 * (k + 1), 1024:2048],
                       sbuf_hi=wkvh[:, k * 2048:k * 2048 + 1024],
                       sbuf_lo=wkvl[:, k * 2048:k * 2048 + 1024])
            wkv_toks.append(emit_split(wqkv_d[128 * k:128 * (k + 1), 2048:3072],
                       sbuf_hi=wkvh[:, k * 2048 + 1024:(k + 1) * 2048],
                       sbuf_lo=wkvl[:, k * 2048 + 1024:(k + 1) * 2048]))
        for k in range(NK):
            emit_split(wqkv_d[128 * k:128 * (k + 1), 0:1024],
                       hi_dst=wqh_d[128 * k:128 * (k + 1), :],
                       lo_dst=wql_d[128 * k:128 * (k + 1), :])
        for k in range(NK):
            emit_split(wout_d[128 * k:128 * (k + 1), :],
                       hi_dst=wouth_d[128 * k:128 * (k + 1), :],
                       lo_dst=woutl_d[128 * k:128 * (k + 1), :])
        tok_wsplit = None
        for k in range(NK):
            for h in range(2):
                tok_splitB_lo = tok_wsplit
                tok_wsplit = emit_split(encw_d[128 * k:128 * (k + 1), 1024 * h:1024 * (h + 1)],
                                        hi_dst=ench_d[128 * k:128 * (k + 1), 1024 * h:1024 * (h + 1)],
                                        lo_dst=encl_d[128 * k:128 * (k + 1), 1024 * h:1024 * (h + 1)])
        if dbg_lnb is not None:
            tcp1 = p.emit("vector", lambda e: e.tensor_copy(costm[:, 2:3], sgn_col[:]),
                          waits=[tok_wsplit, tok_splitB_lo, fr_dbgA], inc="V")
            p.dma("gpsimd", dbg_lnb[:, 2:4], costm[:, 2:4], waits=[tcp1], sem="dst")


        for s in range(nsup):
            apply_toks = apply_cur
            a_ln1h_c, a_ln1l_c = a_ln1X[s % 2]
            for u in range(NSUB):
                if u == 1 and s + 1 < nsup:
                    stn = emit_ln1_stats(s + 1)
                if u == 2 and s + 1 < nsup:
                    apply_nxt, st_nxt = emit_ln1_apply(s + 1, *stn)
                tile = 4 * s + u
                evac_toks = []
                for j in range(4):
                    bank = j % 2
                    mm_tok = None
                    for k in range(NK):
                        for ps_ in range(3):
                            la, rb = [(a_ln1h_c, wkvh), (a_ln1h_c, wkvl), (a_ln1l_c, wkvh)][ps_]
                            mm_tok = p.emit("tensor", lambda e, j=j, k=k, u=u, bank=bank, la=la, rb=rb,
                                            st_=(k == 0 and ps_ == 0),
                                            sp=(k == NK - 1 and ps_ == 2): e.matmul(
                                pb[bank][:], la[:, 512 * k + 128 * u:512 * k + 128 * u + 128],
                                rb[:, 2048 * k + 512 * j:2048 * k + 512 * (j + 1)],
                                start=st_, stop=sp),
                                waits=([apply_toks[k], pfree[bank], wkv_toks[k]] if (k == 0 and ps_ == 0)
                                       else ([apply_toks[k], wkv_toks[k]] if ps_ == 0 else [])),
                                inc="T" if (k == NK - 1 and ps_ == 2) else None)
                    if j < 2:
                        etok = p.emit("vector", lambda e, j=j, bank=bank: e.tensor_copy(
                            k_raw[:, 512 * j:512 * (j + 1)], pb[bank][:]),
                            waits=[mm_tok, kbuf_free], inc="V")
                    else:
                        jj = j - 2
                        vh_view = vxh.rearrange("p (sl c) -> p sl c", c=130)[:, 4 * jj:4 * jj + 4, 0:128]
                        vl_view = vxl.rearrange("p (sl c) -> p sl c", c=130)[:, 4 * jj:4 * jj + 4, 0:128]
                        ps_view = pb[bank].rearrange("p (sl c) -> p sl c", c=128)
                        p.emit("vector", lambda e, o=vh_view, i=ps_view: e.tensor_copy(o, i),
                               waits=[mm_tok, kbuf_free, tok_vext_ones])
                        etok = p.emit("vector", lambda e, o=vl_view, i=ps_view, h=vh_view: e.tensor_tensor(
                            o, i, h, op=OP.subtract), inc="V")
                    evac_toks.append(etok)
                    pfree[bank] = etok
                tok_pi = p.emit("gpsimd", lambda e, tile=tile: e.iota(
                    postm[:], pattern=[[0, 1]], base=128 * tile, channel_multiplier=1,
                    allow_small_or_imprecise_dtypes=True),
                    waits=[tok_invr, fr.get("postm")], inc="G")
                tok_th = p.emit("vector", lambda e: e.tensor_scalar(
                    rtm1[:], invf_rep[:], postm[:], None, op0=OP.mult),
                    waits=[tok_pi, tok_invr, fr.get("tmS"), fr.get("tmC")], inc="V")
                fr["postm"] = tok_th
                ts_tok, tc_tok = rope_sincos(rtm1[:], costm.ap().bitcast(I32), rtm2[:], sintm[:], costm[:],
                                             first_waits=[tok_th])
                tsgn = p.emit("vector", lambda e: e.tensor_scalar(
                    sintm[:, 0:32], sintm[:, 0:32], -1.0, None, op0=OP.mult), waits=[ts_tok], inc="V")
                fr["tmS"] = tsgn
                fr["tmC"] = tc_tok
                k4 = k_raw.rearrange("p (h q j) -> p h q j", q=2, j=32)
                s4 = ks1.rearrange("p (h q j) -> p h q j", q=2, j=32)
                p.emit("vector", lambda e: e.tensor_copy(s4[:, :, 0, :], k4[:, :, 1, :]),
                       waits=[evac_toks[1]])
                tok_ksw = p.emit("vector", lambda e: e.tensor_copy(s4[:, :, 1, :], k4[:, :, 0, :]), inc="V")
                cos_b = costm.ap().unsqueeze(1).to_broadcast([128, 16, 64])
                sin_b = sintm.ap().unsqueeze(1).to_broadcast([128, 16, 64])
                p.emit("vector", lambda e: e.tensor_tensor(
                    ks2.rearrange("p (h j) -> p h j", j=64),
                    k_raw.rearrange("p (h j) -> p h j", j=64), cos_b, op=OP.mult),
                    waits=[tc_tok, evac_toks[1]])
                p.emit("vector", lambda e: e.tensor_tensor(
                    s4.rearrange("p h q j -> p (h q j)"),
                    ks1.rearrange("p (h j) -> p h j", j=64), sin_b, op=OP.mult),
                    waits=[tsgn, tok_ksw])
                tok_kr = p.emit("vector", lambda e: e.tensor_tensor(ks2[:], ks2[:], ks1[:], op=OP.add),
                                inc="V")
                tok_mn = p.emit("vector", lambda e: e.tensor_scalar(ks1[:], ks2[:], 0.0, None, op0=OP.min),
                                inc="V")
                tok_ex = p.emit("scalar", lambda e: e.activation(ks1[:], ks1[:], AF.Exp),
                                waits=[tok_mn], inc="S")
                p.emit("vector", lambda e: e.scalar_tensor_tensor(
                    k_raw[:], ks2[:], 0.0, ks1[:], op0=OP.max, op1=OP.add), waits=[tok_ex])
                p.emit("vector", lambda e: e.tensor_copy(kfh[:], k_raw[:]))
                tok_kf = p.emit("vector", lambda e: e.tensor_tensor(
                    kfl[:], k_raw[:], kfh[:], op=OP.subtract), inc="V")
                for pr in range(8):
                    bank = KVB[pr // 3]
                    off = 129 * (pr % 3)
                    for ps_ in range(3):
                        ka, vb = [(kfh, vxh), (kfh, vxl), (kfl, vxh)][ps_]
                        st_ = first_kv[pr // 3] and ps_ == 0
                        if ps_ == 0:
                            first_kv[pr // 3] = False
                        last = (tile == 4 * nsup - 1) and (pr == 7) and (ps_ == 2)
                        last_kv_mm = p.emit("tensor", lambda e, pr=pr, bank=bank, off=off, st_=st_,
                                            last=last, ka=ka, vb=vb: e.matmul(
                            pb[bank][:, off:off + 129], ka[:, 128 * pr:128 * (pr + 1)],
                            vb[:, 130 * pr:130 * pr + 129], start=st_, stop=last),
                            waits=[tok_kf, evac_toks[3]] if (pr == 0 and ps_ == 0) else [],
                            inc="T" if (pr == 7 and ps_ == 2) else None)
                if tile == (dbg_tile if dbg_tile is not None else 4 * nsup - 1):
                    if dbg_kf is not None:
                        p.dma("gpsimd", dbg_kf, k_raw[:], waits=[tok_kf], sem="dst")
                    if dbg_vx is not None:
                        p.dma("gpsimd", dbg_vx, vxh[:], waits=[evac_toks[3]], sem="dst")
                    if dbg_kraw is not None:
                        p.dma("gpsimd", dbg_kraw, k_raw[:], waits=[evac_toks[1]], sem="dst")
                kbuf_free = last_kv_mm
            ln1_freeP[s % 2] = [last_kv_mm, st_cur[-1]]
            if s + 1 < nsup:
                apply_cur, st_cur = apply_nxt, st_nxt

        kv_evac = []
        for b in range(3):
            nslot = 3 if b < 2 else 2
            kv_evac.append(p.emit("vector", lambda e, b=b, nslot=nslot: e.tensor_copy(
                kv_sb.rearrange("p (sl c) -> p sl c", c=130)[:, 3 * b:3 * b + nslot, 0:129],
                pb[KVB[b]][:, 0:129 * nslot].rearrange("p (sl c) -> p sl c", c=129)),
                waits=[last_kv_mm], inc="V"))
        for b in range(3):
            pfree[KVB[b]] = kv_evac[b]
        tok_bd0 = p.emit("vector", lambda e: e.memset(bd[:], 0.0), inc="V")
        tok_bd = None
        for c in range(NK):
            p.emit("vector", lambda e, c=c: e.tensor_copy(
                bd[0:64, 18 * c:18 * c + 1], kv_sb[0:64, 130 * c + 128:130 * c + 129]),
                waits=[kv_evac[c // 3], tok_bd0])
            tok_bd = p.emit("vector", lambda e, c=c: e.tensor_copy(
                bd[64:128, 18 * c + 1:18 * c + 2], kv_sb[64:128, 130 * c + 128:130 * c + 129]),
                inc="V")
        if dbg_kv is not None:
            p.dma("gpsimd", dbg_kv, kv_sb[:], waits=[kv_evac[-1]], sem="dst")
        if dbg_bd is not None:
            p.dma("gpsimd", dbg_bd, bd[:], waits=[tok_bd], sem="dst")

        if dbg_lnb is not None:
            tcp2 = p.emit("vector", lambda e: e.tensor_copy(costm[:, 4:5], sgn_col[:]),
                          waits=[last_kv_mm], inc="V")
            p.dma("gpsimd", dbg_lnb[:, 4:6], costm[:, 4:6], waits=[tcp2], sem="dst")
        out_store_toks = [last_kv_mm]

        if phases >= 2:
            fr.update({
                "b_ln1": last_kv_mm, "b_xT": last_kv_mm, "b_he": last_kv_mm,
                "b_qf": last_kv_mm, "b_att": last_kv_mm, "b_yT": last_kv_mm,
                "b_spk": last_kv_mm, "contT": None, "accT": None, "gates": None,
                "invden": None, "tanh": None, "logit": None,
            })
            for kk in ("b_enc0", "b_enc1", "b_dec0", "b_dec1", "b_wq0", "b_wq1",
                       "b_wo0", "b_wo1", "b_we10", "b_we11", "b_we20", "b_we21"):
                fr[kk] = last_kv_mm
            out_store_toks = []
            mm2 = None

            def emit_rope(s2):
                tok_th = p.emit("vector", lambda e, s2=s2: e.scalar_tensor_tensor(
                    th1[:], pos0[:], float(SUP * s2), invft[:], op0=OP.add, op1=OP.mult),
                    waits=[fr.get("th1F"), tok_ivt, fr.get("fmC")]
                          + ([last_kv_mm] if s2 == 0 else []), inc="V")
                ci_b = cosrep.ap().bitcast(I32)
                ts_, tc_ = rope_sincos(th1[:], ci_b, th2[:], sinB[:], cosrep[:],
                                       first_waits=[tok_th])
                fr["th1F"] = fr.get("theta_abs")
                sg2 = p.emit("vector", lambda e: e.tensor_scalar(
                    sinB[:], sinB[:], sgn_col[:], None, op0=OP.mult), waits=[ts_, tok_sgn], inc="V")
                fr["fmC"] = tc_
                return ts_, tc_, sg2

            rope_next = emit_rope(0)

            def emit_ln1_loads(s2):
                toks = []
                for k in range(NK):
                    p.dma("sync", lnh[:, 512 * k:512 * (k + 1)],
                          ln1h_d[128 * k:128 * (k + 1), SUP * s2:SUP * (s2 + 1)],
                          waits=[fr["b_ln1"], ln1_store[s2]] if k == 0 else (), sem="Lln")
                    toks.append(p.dma("sync", lnl[:, 512 * k:512 * (k + 1)],
                                      ln1l_d[128 * k:128 * (k + 1), SUP * s2:SUP * (s2 + 1)],
                                      sem="Lln"))
                return toks

            ln1_next = emit_ln1_loads(0)

            for s in range(nsup):
                ts_tok, tc_tok, tok_sgn2 = rope_next
                ln1_toks = ln1_next
                xt_toks = []
                for k in range(NK):
                    xt_toks.append(p.dma("sync", b_xT[:, 512 * k:512 * (k + 1)],
                                         xT_d[128 * k:128 * (k + 1), SUP * s:SUP * (s + 1)],
                                         waits=[fr["b_xT"], fr["b_he"]] if k == 0 else (),
                                         sem="LxT"))
                qf_toks = []
                q_mm = None
                for n in range(NK):
                    wq16 = b_wq[n % 2].bitcast(mybir.dt.bfloat16)
                    p.dma("sync", wq16[:, 0:1024].rearrange("p (k n) -> p k n", n=128),
                          wqh_d[:, 128 * n:128 * (n + 1)].rearrange("(k p) n -> p k n", p=128),
                          waits=([fr[f"b_wq{n % 2}"], tok_wsplit, tok_splitB_lo] if n < 2
                                 else [fr[f"b_wq{n % 2}"]]),
                          sem=f"Lwq{n % 2}")
                    wqt = p.dma("sync", wq16[:, 1024:2048].rearrange("p (k n) -> p k n", n=128),
                                wql_d[:, 128 * n:128 * (n + 1)].rearrange("(k p) n -> p k n", p=128),
                                sem=f"Lwq{n % 2}")
                    for k in range(NK):
                        for ps_ in range(3):
                            lo_, ro_ = [(0, lnh), (0, lnl), (1024, lnh)][ps_]
                            q_mm = p.emit("tensor", lambda e, n=n, k=k, lo_=lo_, ro_=ro_,
                                          wq16=wq16,
                                          st_=(k == 0 and ps_ == 0),
                                          sp=(k == NK - 1 and ps_ == 2): e.matmul(
                                pb[n % 2][:], wq16[:, lo_ + 128 * k:lo_ + 128 * (k + 1)],
                                ro_[:, 512 * k:512 * (k + 1)], start=st_, stop=sp),
                                waits=([wqt, ln1_toks[k], pfree[n % 2]] if (k == 0 and ps_ == 0)
                                       else ([ln1_toks[k]] if ps_ == 0 else [])),
                                inc="T" if (k == NK - 1 and ps_ == 2) else None)
                    fr[f"b_wq{n % 2}"] = q_mm
                    if s == 0 and n == 0 and dbg_wq is not None:
                        fr[f"b_wq{n % 2}"] = p.dma("gpsimd", dbg_wq[:, 0:1024], b_wq[0],
                                                   waits=[q_mm], sem="dst")
                    if s == 0 and n < 2 and dbg_qraw is not None:
                        tqr = p.emit("vector", lambda e, n=n: e.tensor_copy(
                            rs3[:], pb[n % 2][:]), waits=[q_mm], inc="V")
                        fr["qraw_dbg"] = p.dma("gpsimd", dbg_qraw[:, 512 * n:512 * (n + 1)],
                                               rs3[:], waits=[tqr], sem="dst")
                    p.emit("vector", lambda e, n=n: e.tensor_tensor(rs1[:], pb[n % 2][:], cosrep[:],
                                                                    op=OP.mult),
                           waits=[q_mm, tc_tok, fr.get("qraw_dbg")]
                                 + ([fr.get("sqmm")] if n == 0 else []))
                    for off in (0, 32, 64, 96):
                        sw = off ^ 32
                        p.emit("vector", lambda e, n=n, off=off, sw=sw: e.tensor_tensor(
                            rs3[off:off + 32, :], pb[n % 2][sw:sw + 32, :], sinB[off:off + 32, :],
                            op=OP.mult),
                            waits=[tok_sgn2, tc_tok, fr.get("qraw_dbg")] if off == 0 else [])
                    ev = p.emit("vector", lambda e: e.tensor_tensor(rs1[:], rs1[:], rs3[:], op=OP.add),
                                inc="V")
                    pfree[n % 2] = ev
                    tok_mn = p.emit("vector", lambda e: e.tensor_scalar(
                        rs3[:], rs1[:], 0.125, 0.0, op0=OP.mult, op1=OP.min), inc="V")
                    tok_ex = p.emit("scalar", lambda e: e.activation(rs3[:], rs3[:], AF.Exp),
                                    waits=[tok_mn], inc="S")
                    p.emit("vector", lambda e: e.tensor_scalar(rs1[:], rs1[:], 0.125, 0.0,
                                                               op0=OP.mult, op1=OP.max))
                    qf_toks.append(p.emit("vector", lambda e, n=n: e.tensor_tensor(
                        b_qf[:, 512 * n:512 * (n + 1)], rs1[:], rs3[:], op=OP.add),
                        waits=[tok_ex] + ([fr["b_qf"]] if n == 0 else []), inc="V"))
                fr["b_ln1"] = q_mm
                if s + 1 < nsup:
                    rope_next = emit_rope(s + 1)
                if dbg_qf is not None:
                    for k in range(NK):
                        fr["qf_dbg"] = p.dma("gpsimd", dbg_qf[128 * k:128 * (k + 1), SUP * s:SUP * (s + 1)],
                              b_qf[:, 512 * k:512 * (k + 1)], waits=[qf_toks[-1]], sem="dst")

                den_mm = None
                for k in range(NK):
                    den_mm = p.emit("tensor", lambda e, k=k: e.matmul(
                        pb[4][0:16, :], bd[:, 16 * k:16 * k + 16], b_qf[:, 512 * k:512 * (k + 1)],
                        start=(k == 0), stop=(k == NK - 1)),
                        waits=[qf_toks[k], tok_bd] + ([pfree[4]] if k == 0 else []),
                        inc="T" if k == NK - 1 else None)
                p.emit("vector", lambda e: e.tensor_scalar(rs1[0:16, :], pb[4][0:16, :], 1e-6, None,
                                                           op0=OP.max), waits=[den_mm])
                tok_den = p.emit("vector", lambda e: e.reciprocal(
                    rowp[0:16, C_ID:C_ID + 512], rs1[0:16, :]), waits=[fr["invden"]], inc="V")
                pfree[4] = tok_den
                if dbg_den is not None:
                    fr["invden"] = p.dma("gpsimd", dbg_den[:, SUP * s:SUP * (s + 1)], rowp[0:16, C_ID:C_ID + 512],
                          waits=[tok_den], sem="dst")

                att_toks = []
                inv_last = None
                for pr in range(8):
                    ib = 2 + (pr % 2)
                    inv_tok = p.emit("tensor", lambda e, pr=pr, ib=ib: e.matmul(
                        pb[ib][:], psel16[0:16, 128 * pr:128 * (pr + 1)],
                        rowp[0:16, C_ID:C_ID + 512], start=True, stop=True),
                        waits=[tok_den, tok_psel, pfree[ib]], inc="T")
                    inv_last = inv_tok
                    mb = pr % 2
                    p.emit("tensor", lambda e, pr=pr, mb=mb: e.matmul(
                        pb[mb][0:64, :], kv_sb[0:64, 130 * pr:130 * pr + 64],
                        b_qf[0:64, 512 * pr:512 * (pr + 1)], start=True, stop=True),
                        waits=[kv_evac[pr // 3], qf_toks[pr], pfree[mb]])
                    att_mm = p.emit("tensor", lambda e, pr=pr, mb=mb: e.matmul(
                        pb[mb][64:128, :], kv_sb[64:128, 130 * pr + 64:130 * pr + 128],
                        b_qf[64:128, 512 * pr:512 * (pr + 1)], start=True, stop=True), inc="T")
                    p.emit("vector", lambda e, ib=ib: e.tensor_copy(rs2[:], pb[ib][:]),
                           waits=[inv_tok])
                    t_att = p.emit("vector", lambda e, pr=pr, mb=mb: e.tensor_tensor(
                        rs1[:], pb[mb][:], rs2[:], op=OP.mult),
                        waits=[att_mm] + ([fr["b_att"]] if pr == 0 else []), inc="V")
                    t_h = p.emit("vector", lambda e, pr=pr: e.tensor_copy(
                        atth[:, 512 * pr:512 * (pr + 1)], rs1[:]), waits=[t_att], inc="V")
                    etok = p.emit("vector", lambda e, pr=pr: e.tensor_tensor(
                        attl[:, 512 * pr:512 * (pr + 1)], rs1[:], atth[:, 512 * pr:512 * (pr + 1)],
                        op=OP.subtract), waits=[t_h], inc="V")
                    att_toks.append(etok)
                    pfree[mb] = etok
                    pfree[ib] = etok
                fr["b_qf"] = att_toks[-1]
                fr["invden"] = inv_last
                if dbg_att is not None:
                    for k in range(NK):
                        fr["att_dbg"] = p.dma("gpsimd", dbg_att[128 * k:128 * (k + 1), SUP * s:SUP * (s + 1)],
                              b_att[:, 512 * k:512 * (k + 1)], waits=[att_toks[-1]], sem="dst")

                y_toks = []
                wo_mm = None
                for n in range(NK):
                    wo16 = b_wo[n % 2].bitcast(mybir.dt.bfloat16)
                    p.dma("sync", wo16[:, 0:1024].rearrange("p (k n) -> p k n", n=128),
                          wouth_d[:, 128 * n:128 * (n + 1)].rearrange("(k p) n -> p k n", p=128),
                          waits=[fr[f"b_wo{n % 2}"]], sem=f"Lwo{n % 2}")
                    wot = p.dma("sync", wo16[:, 1024:2048].rearrange("p (k n) -> p k n", n=128),
                                woutl_d[:, 128 * n:128 * (n + 1)].rearrange("(k p) n -> p k n", p=128),
                                sem=f"Lwo{n % 2}")
                    for k in range(NK):
                        for ps_ in range(3):
                            lo_, ro_ = [(0, atth), (0, attl), (1024, atth)][ps_]
                            wo_mm = p.emit("tensor", lambda e, n=n, k=k, lo_=lo_, ro_=ro_,
                                           wo16=wo16,
                                           st_=(k == 0 and ps_ == 0),
                                           sp=(k == NK - 1 and ps_ == 2): e.matmul(
                                pb[n % 2][:], wo16[:, lo_ + 128 * k:lo_ + 128 * (k + 1)],
                                ro_[:, 512 * k:512 * (k + 1)], start=st_, stop=sp),
                                waits=([wot, att_toks[k], pfree[n % 2]] if (k == 0 and ps_ == 0)
                                       else ([att_toks[k]] if ps_ == 0 else [])),
                                inc="T" if (k == NK - 1 and ps_ == 2) else None)
                    fr[f"b_wo{n % 2}"] = wo_mm
                    ytok = p.emit("vector", lambda e, n=n: e.tensor_tensor(
                        b_yT[:, 512 * n:512 * (n + 1)], pb[n % 2][:], b_xT[:, 512 * n:512 * (n + 1)],
                        op=OP.add),
                        waits=[wo_mm, xt_toks[n]] + ([fr["b_yT"]] if n == 0 else []), inc="V")
                    y_toks.append(ytok)
                    pfree[n % 2] = ytok
                fr["b_att"] = wo_mm
                fr["b_xT"] = y_toks[-1]
                if dbg_y is not None:
                    for k in range(NK):
                        fr["y_dbg"] = p.dma("gpsimd", dbg_y[128 * k:128 * (k + 1), SUP * s:SUP * (s + 1)],
                              b_yT[:, 512 * k:512 * (k + 1)], waits=[y_toks[-1]], sem="dst")
                    fr["b_yT"] = fr["y_dbg"]

                chunks = [(b_yT[:, 512 * k:512 * (k + 1)], y_toks[k]) for k in range(NK)]
                tok_a, tok_b = ln_fm(chunks, 4, 2, 3)
                ln2_toks = []
                for k in range(NK):
                    p.emit("vector", lambda e, k=k: e.tensor_tensor(
                        rs3[:], b_yT[:, 512 * k:512 * (k + 1)], abc[:], op=OP.mult),
                        waits=[tok_a, fr["b_ln1"]] if k == 0 else [])
                    t_ln = p.emit("vector", lambda e, k=k: e.tensor_tensor(rs3[:], rs3[:], bbc[:], op=OP.add),
                                  waits=[tok_b] if k == 0 else [], inc="V")
                    t_h = p.emit("vector", lambda e, k=k: e.tensor_copy(
                        lnh[:, 512 * k:512 * (k + 1)], rs3[:]), waits=[t_ln], inc="V")
                    ln2_toks.append(p.emit("vector", lambda e, k=k: e.tensor_tensor(
                        lnl[:, 512 * k:512 * (k + 1)], rs3[:], lnh[:, 512 * k:512 * (k + 1)],
                        op=OP.subtract), waits=[t_h], inc="V"))
                if dbg_ln2 is not None:
                    for k in range(NK):
                        tdl = p.emit("vector", lambda e, k=k: e.tensor_tensor(
                            rs3[:], lnh[:, 512 * k:512 * (k + 1)], lnl[:, 512 * k:512 * (k + 1)],
                            op=OP.add), waits=[ln2_toks[-1], fr.get("ln2_dbg")], inc="V")
                        fr["ln2_dbg"] = p.dma("gpsimd", dbg_ln2[128 * k:128 * (k + 1), SUP * s:SUP * (s + 1)],
                              rs3[:], waits=[tdl], sem="dst")

                tok_cont = None
                for g in range(2):
                    enc_last = [None] * 8
                    for k in range(NK):
                        e16 = b_enc[k % 2].bitcast(mybir.dt.bfloat16)
                        p.dma("sync", e16[:, 0:2048], ench_d[128 * k:128 * (k + 1), :],
                              waits=[fr[f"b_enc{k % 2}"]] + ([tok_wsplit, tok_splitB_lo]
                                                              if (s == 0 and g == 0 and k < 2) else []),
                              sem=f"Len{k % 2}")
                        et = p.dma("sync", e16[:, 2048:4096], encl_d[128 * k:128 * (k + 1), :],
                                   sem=f"Len{k % 2}")
                        for s8 in range(8):
                            sg = 8 * g + s8
                            for ps_ in range(3):
                                wof, ro_ = [(0, lnh), (0, lnl), (2048, lnh)][ps_]
                                enc_last[s8] = p.emit("tensor", lambda e, k=k, s8=s8, sg=sg,
                                                      wof=wof, ro_=ro_, e16=e16,
                                                      st_=(k == 0 and ps_ == 0),
                                                      sp=(k == NK - 1 and ps_ == 2): e.matmul(
                                    pb[s8][:], e16[:, wof + 128 * sg:wof + 128 * (sg + 1)],
                                    ro_[:, 512 * k:512 * (k + 1)], start=st_, stop=sp),
                                    waits=([et, ln2_toks[k], pfree[s8]] if (k == 0 and ps_ == 0)
                                           else ([et, ln2_toks[k]] if ps_ == 0 else [])),
                                    inc="T" if (s8 == 7 or (k == NK - 1 and ps_ == 2)) else None)
                        fr[f"b_enc{k % 2}"] = enc_last[7]
                    if s == 0 and g == 0 and dbg_encp is not None:
                        tdbg = p.emit("vector", lambda e: e.tensor_copy(rs3[:], pb[0][:]),
                                      waits=[enc_last[0]], inc="V")
                        p.dma("gpsimd", dbg_encp, rs3[:], waits=[tdbg], sem="dst")
                        p.dma("gpsimd", dbg_ln2w, b_ln2[:, 0:512], waits=[enc_last[0]], sem="dst")
                        p.dma("gpsimd", dbg_encw, b_enc[1][:], waits=[enc_last[7]], sem="dst")
                    spk_toks = []
                    for s8 in range(8):
                        sg = 8 * g + s8
                        spk_toks.append(p.emit("scalar", lambda e, s8=s8, sg=sg: e.activation(
                            b_spk[:, 512 * s8:512 * (s8 + 1)], pb[s8][:], AF.Sigmoid,
                            bias=encb[:, sg:sg + 1], scale=1.0),
                            waits=[enc_last[s8]] + ([fr["b_spk"], tok_wsplit, tok_splitB_lo]
                                                    if s8 == 0 else []), inc="S"))
                    if s == 0 and dbg_spk is not None:
                        for s8 in range(8):
                            p.dma("gpsimd", dbg_spk[:, 512 * (8 * g + s8) - 4096 * g if False else 512 * s8 + 0:512 * s8 + 512] if False else dbg_spk[:, 2048 * g + 256 * s8:2048 * g + 256 * s8 + 256],
                                  b_spk[:, 512 * s8:512 * s8 + 256], waits=[spk_toks[s8]], sem="dst")
                    s2c_mm = None
                    for s8 in range(8):
                        sg = 8 * g + s8
                        s2c_mm = p.emit("tensor", lambda e, s8=s8, sg=sg: e.matmul(
                            pb[0][0:64, :], s2cw[:, 64 * sg:64 * (sg + 1)],
                            b_spk[:, 512 * s8:512 * (s8 + 1)], start=(s8 == 0), stop=(s8 == 7)),
                            waits=[spk_toks[s8]] + ([spk_toks[0], tok_wsmall] if s8 == 0 else []),
                            inc="T" if s8 == 7 else None)
                    if g == 0:
                        tok_cont = p.emit("vector", lambda e: e.tensor_copy(contT[:], pb[0][0:64, :]),
                                          waits=[s2c_mm, fr["contT"]], inc="V")
                    else:
                        p.emit("vector", lambda e: e.tensor_tensor(contT[:], contT[:], pb[0][0:64, :],
                                                                   op=OP.add), waits=[s2c_mm])
                        tok_cont = p.emit("vector", lambda e: e.tensor_scalar(
                            contT[:], contT[:], s2cb[:], None, op0=OP.add), inc="V")
                    for i in range(8):
                        pfree[i] = tok_cont
                fr["b_ln1"] = fr.get("ln2_dbg") or enc_last[7]
                if s + 1 < nsup:
                    ln1_next = emit_ln1_loads(s + 1)
                if dbg_cont is not None:
                    fr["contT"] = p.dma("gpsimd", dbg_cont[:, SUP * s:SUP * (s + 1)], contT[:], waits=[tok_cont],
                          sem="dst")

                r1 = p.emit("tensor", lambda e: e.matmul(pb[4][0:64, :], rw1[:], contT[:],
                                                         start=True, stop=True),
                            waits=[tok_cont, tok_wsmall, pfree[4]], inc="T")
                rt = p.emit("scalar", lambda e: e.activation(rowp[64:128, C_RT:C_RT + 512], pb[4][0:64, :],
                                                             AF.Tanh, bias=rb1t[64:128, :], scale=1.0),
                            waits=[r1, fr["tanh"], fr.get("tanh_dbg")], inc="S")
                r2 = p.emit("tensor", lambda e: e.matmul(pb[4][0:8, :], rw2t[64:128, :],
                                                         rowp[64:128, C_RT:C_RT + 512],
                                                         start=True, stop=True), waits=[rt], inc="T")
                fr["tanh"] = r2
                rl = p.emit("vector", lambda e: e.tensor_scalar(rowp[32:40, C_RT:C_RT + 512],
                                                                pb[4][0:8, :], rb2[:], None, op0=OP.add),
                            waits=[r2, fr["logit"], fr.get("lgf_dbg")], inc="V")
                if s == 0 and dbg_tanh is not None:
                    fr["tanh_dbg"] = p.dma("gpsimd", dbg_tanh, rowp[64:128, C_RT:C_RT + 512],
                                           waits=[rl], sem="dst")
                    fr["lgf_dbg"] = p.dma("gpsimd", dbg_lgf, rowp[32:40, C_RT:C_RT + 512],
                                          waits=[rl], sem="dst")
                pfree[4] = rl
                tr_last = None
                for u in range(4):
                    tr_last = p.emit("tensor", lambda e, u=u: e.transpose(
                        pb[4][:, 8 * u:8 * u + 8], rowp[32:40, C_RT + 128 * u:C_RT + 128 * (u + 1)],
                        ident[32:40, 32:40]),
                        waits=[rl, tok_ident] + ([pfree[4]] if u == 0 else []), inc="T")
                fr["logit"] = tr_last
                tg = p.emit("vector", lambda e: e.tensor_copy(gtm[:], pb[4][:, 0:32]),
                            waits=[tr_last, fr.get("gtm2_dbg")], inc="V")
                pfree[4] = tg
                if s == 0 and dbg_lgtm is not None:
                    fr["lgtm_dbg"] = p.dma("gpsimd", dbg_lgtm, gtm[:], waits=[tg], sem="dst")
                g3 = gtm.ap().rearrange("p (u v) -> p u v", v=8)
                tok_ex = p.emit("scalar", lambda e: e.activation(rs2[:, 0:32], gtm[:], AF.Exp),
                                waits=[tg, fr.get("rs2V"), fr.get("lgtm_dbg")], inc="S")
                for u in range(4):
                    p.emit("vector", lambda e, u=u: e.max(rs1[:, 8 * u:8 * (u + 1)],
                                                          rs2[:, 8 * u:8 * (u + 1)]),
                           waits=[tok_ex] if u == 0 else [])
                for u in range(4):
                    p.emit("vector", lambda e, u=u: e.tensor_scalar(
                        gtm[:, 8 * u:8 * (u + 1)], rs2[:, 8 * u:8 * (u + 1)],
                        rs1[:, 8 * u + 1:8 * u + 2], None, op0=OP.is_ge))
                p.emit("vector", lambda e: e.tensor_tensor(rs2[:, 0:32], rs2[:, 0:32], gtm[:],
                                                           op=OP.mult))
                for u in range(4):
                    p.emit("vector", lambda e, u=u: e.tensor_tensor(
                        rs1[:, 8 * u + 2:8 * u + 3], rs1[:, 8 * u:8 * u + 1],
                        rs1[:, 8 * u + 1:8 * u + 2], op=OP.add))
                for u in range(4):
                    p.emit("vector", lambda e, u=u: e.reciprocal(rs1[:, 8 * u + 3:8 * u + 4],
                                                                 rs1[:, 8 * u + 2:8 * u + 3]))
                for u in range(4):
                    tok_gt = p.emit("vector", lambda e, u=u: e.tensor_scalar(
                        gtm[:, 8 * u:8 * (u + 1)], rs2[:, 8 * u:8 * (u + 1)],
                        rs1[:, 8 * u + 3:8 * u + 4], None, op0=OP.mult),
                        inc="V" if u == 3 else None)
                if s == 0 and dbg_gtm2 is not None:
                    fr["gtm2_dbg"] = p.dma("gpsimd", dbg_gtm2, gtm[:], waits=[tok_gt], sem="dst")
                gt_last = None
                for u in range(4):
                    gt_last = p.emit("tensor", lambda e, u=u: e.transpose(
                        pb[4][0:8, 128 * u:128 * (u + 1)], gtm[:, 8 * u:8 * u + 8], ident[:]),
                        waits=[tok_gt, fr.get("gtm2_dbg")] + ([pfree[4]] if u == 0 else []), inc="T")
                tok_gatesT = p.emit("vector", lambda e: e.tensor_copy(
                    rowp[0:8, C_RT:C_RT + 512], pb[4][0:8, :]),
                    waits=[gt_last, fr["gates"]], inc="V")
                pfree[4] = tok_gatesT
                if dbg_gat is not None:
                    fr["gates"] = p.dma("gpsimd", dbg_gat[:, SUP * s:SUP * (s + 1)], rowp[0:8, C_RT:C_RT + 512],
                          waits=[tok_gatesT], sem="dst")

                bt = p.emit("tensor", lambda e: e.matmul(pb[4][0:64, :], be2[:],
                                                         rowp[0:8, C_RT:C_RT + 512],
                                                         start=True, stop=True),
                            waits=[tok_gatesT, pfree[4]], inc="T")
                tok_acc = p.emit("vector", lambda e: e.tensor_copy(accT[:], pb[4][0:64, :]),
                                 waits=[bt, fr["accT"]], inc="V")
                pfree[4] = tok_acc
                gates_last = bt
                for e_ in range(8):
                    eb = e_ % 2
                    w1t = p.dma("gpsimd", b_we1[eb], we1_d[e_], waits=[fr[f"b_we1{eb}"]], sem=f"Lw1{eb}")
                    w2t = p.dma("gpsimd", b_we2[eb].rearrange("p (c m) -> p c m", m=64),
                                we2_d[e_].rearrange("(c p) m -> p c m", p=128),
                                waits=[fr[f"b_we2{eb}"]], sem=f"Lw2{eb}")
                    he_toks = []
                    for hk in range(NK):
                        hb = hk % 2
                        hm = p.emit("tensor", lambda e, hk=hk, eb=eb, hb=hb: e.matmul(
                            pb[hb][:], b_we1[eb][:, 128 * hk:128 * (hk + 1)], contT[:],
                            start=True, stop=True),
                            waits=[w1t, tok_cont, pfree[hb]] +
                                  ([fr["b_he"]] if (e_ == 0 and hk == 0) else []),
                            inc="T")
                        het = p.emit("scalar", lambda e, e_=e_, hk=hk, hb=hb: e.activation(
                            b_he[:, 512 * hk:512 * (hk + 1)], pb[hb][:], AF.Relu,
                            bias=be1c[:, 8 * e_ + hk:8 * e_ + hk + 1], scale=1.0),
                            waits=[hm], inc="S")
                        he_toks.append(het)
                        pfree[hb] = het
                    fr[f"b_we1{eb}"] = he_toks[-1]
                    rlo = 64 * (e_ % 2)
                    mm2 = None
                    for hk in range(NK):
                        mm2 = p.emit("tensor", lambda e, hk=hk, eb=eb, rlo=rlo: e.matmul(
                            pb[6][rlo:rlo + 64, :], b_we2[eb][:, 64 * hk:64 * (hk + 1)],
                            b_he[:, 512 * hk:512 * (hk + 1)], start=(hk == 0), stop=(hk == NK - 1)),
                            waits=[w2t, he_toks[hk]] +
                                  ([pfree[6]] if (hk == 0 and e_ % 2 == 0) else []),
                            inc="T" if hk == NK - 1 else None)
                    fr[f"b_we2{eb}"] = mm2
                    fr["b_he"] = mm2
                    if e_ % 2 == 1:
                        gb = p.emit("tensor", lambda e, e_=e_: e.matmul(
                            pb[7][:], psel16[0:8, 128 * (e_ // 2):128 * (e_ // 2 + 1)],
                            rowp[0:8, C_RT:C_RT + 512],
                            start=True, stop=True), waits=[tok_gatesT, pfree[7]], inc="T")
                        gates_last = gb
                        p.emit("vector", lambda e: e.tensor_copy(rs2[:], pb[7][:]),
                               waits=[gb])
                        gm = p.emit("vector", lambda e: e.tensor_tensor(rs1[:], pb[6][:], rs2[:],
                                                                        op=OP.mult),
                                    waits=[mm2, tok_acc], inc="V")
                        pfree[7] = gm
                        pfree[6] = gm
                        p.emit("vector", lambda e: e.tensor_tensor(accT[:], accT[:], rs1[0:64, :],
                                                                   op=OP.add))
                        p.emit("vector", lambda e: e.tensor_copy(rs3[0:64, :], rs1[64:128, :]))
                        tok_acc = p.emit("vector", lambda e: e.tensor_tensor(
                            accT[:], accT[:], rs3[0:64, :], op=OP.add), inc="V")
                fr["gates"] = gates_last
                fr["contT"] = mm2
                if dbg_acc is not None:
                    fr["accT"] = p.dma("gpsimd", dbg_acc[:, SUP * s:SUP * (s + 1)], accT[:], waits=[tok_acc],
                          sem="dst")

                dec_toks = []
                acc_last = None
                moe_toks = {}
                b_moeX = [b_moe, b_moe1]
                for h2 in range(2):
                    for sg in range(16):
                        cb = sg % 2
                        cm = p.emit("tensor", lambda e, sg=sg, h2=h2, cb=cb: e.matmul(
                            pb[cb][:, 0:256], b_c2s[:, 128 * sg:128 * (sg + 1)],
                            accT[0:64, 256 * h2:256 * (h2 + 1)], start=True, stop=True),
                            waits=[tok_acc, tok_c2sw, pfree[cb]], inc="T")
                        acc_last = cm
                        mt = p.emit("scalar", lambda e, sg=sg, cb=cb, h2=h2: e.activation(
                            b_moeX[h2][:, 256 * sg:256 * (sg + 1)], pb[cb][:, 0:256], AF.Sigmoid,
                            bias=c2sb[:, sg:sg + 1], scale=1.0),
                            waits=[cm] + ([fr["b_spk"]] if (h2 == 0 and sg == 0) else [])
                                 + ([fr["b_att"], fr.get("att_dbg")] if (h2 == 1 and sg == 0) else []),
                            inc="S")
                        moe_toks[(h2, sg)] = mt
                        pfree[cb] = mt
                dec_mm = [None] * 8
                for sg in range(16):
                    db = sg % 2
                    dt_ = p.dma("gpsimd", b_dec[db], decw_d[128 * sg:128 * (sg + 1), :],
                                waits=[fr[f"b_dec{db}"]], sem=f"Ldc{db}")
                    for dk in range(8):
                        for h2 in range(2):
                            dec_mm[dk] = p.emit("tensor", lambda e, sg=sg, dk=dk, db=db, h2=h2: e.matmul(
                                pb[dk][:, 256 * h2:256 * (h2 + 1)], b_dec[db][:, 128 * dk:128 * (dk + 1)],
                                b_moeX[h2][:, 256 * sg:256 * (sg + 1)],
                                start=(sg == 0 and h2 == 0), stop=(sg == 15 and h2 == 1)),
                                waits=([dt_, moe_toks[(0, sg)], moe_toks[(1, sg)]] if (dk == 0 and h2 == 0) else [])
                                      + ([pfree[dk]] if (sg == 0 and h2 == 0) else []),
                                inc="T" if (dk == 7 or (sg == 15 and h2 == 1)) else None)
                    fr[f"b_dec{db}"] = dec_mm[7]
                for dk in range(8):
                    dct = p.emit("scalar", lambda e, dk=dk: e.activation(
                        b_decT[:, 512 * dk:512 * (dk + 1)],
                        pb[dk][:], AF.Sigmoid, bias=decb[:, dk:dk + 1], scale=1.0),
                        waits=[dec_mm[dk]] + ([fr["b_qf"], fr.get("qf_dbg"), fr.get("lnb_dbg")]
                                              if dk == 0 else []),
                        inc="S")
                    dec_toks.append(dct)
                    pfree[dk] = dct
                fr["b_spk"] = dec_mm[7]
                fr["b_moe1"] = dec_mm[7]
                fr["accT"] = acc_last
                tok_dec = dec_toks[-1]
                if dbg_dec is not None:
                    for k in range(NK):
                        fr["dec_dbg"] = p.dma("gpsimd", dbg_dec[128 * k:128 * (k + 1), SUP * s:SUP * (s + 1)],
                              b_decT[:, 512 * k:512 * (k + 1)], waits=[tok_dec], sem="dst")

                chunks = [(b_decT[:, 512 * k:512 * (k + 1)], tok_dec) for k in range(NK)]
                tok_a, tok_b = ln_fm(chunks, 4, 2, 3)
                o_toks = []
                for k in range(NK):
                    p.emit("vector", lambda e, k=k: e.tensor_tensor(
                        rs3[:], b_decT[:, 512 * k:512 * (k + 1)], abc[:], op=OP.mult),
                        waits=[tok_a, fr["b_att"]] if k == 0 else [])
                    p.emit("vector", lambda e, k=k: e.tensor_tensor(rs3[:], rs3[:], bbc[:], op=OP.add),
                           waits=[tok_b] if k == 0 else [])
                    p.emit("vector", lambda e, k=k: e.tensor_scalar(
                        rs3[:], rs3[:], lng[:, k:k + 1], lnb[:, k:k + 1], op0=OP.mult, op1=OP.add))
                    o_toks.append(p.emit("vector", lambda e, k=k: e.tensor_tensor(
                        b_outT[:, 512 * k:512 * (k + 1)], rs3[:], b_yT[:, 512 * k:512 * (k + 1)],
                        op=OP.add), waits=[y_toks[k], fr.get("att_dbg")], inc="V"))
                st_last = None
                for k in range(NK):
                    st_last = p.dma("gpsimd", outT_d[128 * k:128 * (k + 1), SUP * s:SUP * (s + 1)],
                                    b_outT[:, 512 * k:512 * (k + 1)], waits=[o_toks[k]], sem="dst")
                    out_store_toks.append(st_last)
                fr["b_att"] = st_last
                fr["b_yT"] = o_toks[-1]
                fr["b_qf"] = fr.get("dec_dbg") or o_toks[-1]
                fr["rs2V"] = o_toks[-1]

        p.emit("gpsimd", None, waits=[("dst", p.emitted["dst"], None)])
        p.simulate()
        p.replay(block)

    return nc, dbg_t



_NC_CACHE = {}


def _get_nc():
    if "nc" not in _NC_CACHE:
        _NC_CACHE["nc"] = build_core(debug=(), phases=2)[0]
    return _NC_CACHE["nc"]


def kernel(L0_attn_x, L0_attn_W_qkv, L0_attn_W_out, L0_ffn_enc_W, L0_ffn_enc_b,
           L0_ffn_s2c_W, L0_ffn_s2c_b, L0_ffn_rW1, L0_ffn_rb1, L0_ffn_rW2, L0_ffn_rb2,
           L0_ffn_We1, L0_ffn_be1, L0_ffn_We2, L0_ffn_be2, L0_ffn_c2s_W, L0_ffn_c2s_b,
           L0_ffn_dec_W, L0_ffn_dec_b, L0_ffn_ln_g, L0_ffn_ln_b):
    x = np.asarray(L0_attn_x, dtype=np.float32)
    weights = {
        "wqkv": np.ascontiguousarray(np.asarray(L0_attn_W_qkv, np.float32)),
        "wout": np.ascontiguousarray(np.asarray(L0_attn_W_out, np.float32)),
        "encw": np.ascontiguousarray(np.asarray(L0_ffn_enc_W, np.float32)),
        "encb": np.ascontiguousarray(np.asarray(L0_ffn_enc_b, np.float32)),
        "s2cw": np.ascontiguousarray(np.asarray(L0_ffn_s2c_W, np.float32)),
        "s2cb": np.ascontiguousarray(np.asarray(L0_ffn_s2c_b, np.float32)),
        "rw1": np.ascontiguousarray(np.asarray(L0_ffn_rW1, np.float32)),
        "rb1": np.ascontiguousarray(np.asarray(L0_ffn_rb1, np.float32)),
        "rw2": np.ascontiguousarray(np.asarray(L0_ffn_rW2, np.float32)),
        "rb2": np.ascontiguousarray(np.asarray(L0_ffn_rb2, np.float32)),
        "we1": np.ascontiguousarray(np.asarray(L0_ffn_We1, np.float32)),
        "be1": np.ascontiguousarray(np.asarray(L0_ffn_be1, np.float32)),
        "we2": np.ascontiguousarray(np.asarray(L0_ffn_We2, np.float32)),
        "be2": np.ascontiguousarray(np.asarray(L0_ffn_be2, np.float32)),
        "c2sw": np.ascontiguousarray(np.asarray(L0_ffn_c2s_W, np.float32)),
        "c2sb": np.ascontiguousarray(np.asarray(L0_ffn_c2s_b, np.float32)),
        "decw": np.ascontiguousarray(np.asarray(L0_ffn_dec_W, np.float32)),
        "decb": np.ascontiguousarray(np.asarray(L0_ffn_dec_b, np.float32)),
        "lng": np.ascontiguousarray(np.asarray(L0_ffn_ln_g, np.float32)),
        "lnb": np.ascontiguousarray(np.asarray(L0_ffn_ln_b, np.float32)),
    }
    n_cores = 8
    in_maps = []
    for c in range(n_cores):
        im = dict(weights)
        im["xT"] = np.ascontiguousarray(x[c].T)
        in_maps.append(im)
    nc = _get_nc()
    res = run_bass_kernel_spmd(nc, in_maps, core_ids=list(range(n_cores)))
    out = np.empty((n_cores, T, D), dtype=np.float32)
    for c in range(n_cores):
        out[c] = np.asarray(res.results[c]["outT"]).T
    return out


# version 19
# speedup vs baseline: 1.0020x; 1.0020x over previous
import numpy as np
import concourse.bass as bass
import concourse.mybir as mybir
from concourse.bass_utils import run_bass_kernel_spmd
from concourse.alu_op_type import AluOpType as OP

F32 = mybir.dt.float32
I32 = mybir.dt.int32
AF = mybir.ActivationFunctionType

T = 4096
D = 1024
NSUP = 8
SUP = 512
NSUB = 4
NK = D // 128
EH = 2048
ROPE_BASE = 500000.0
TWO_PI = 2.0 * np.pi
CW1 = float(np.float32(6.28125))
CW2 = float(np.float32(TWO_PI - 6.28125))
PI = float(np.pi)
LNB32 = float(-np.log(ROPE_BASE) / 32.0)
LNEPS = 1e-5
BIGNEG = -2.0


class Prog:
    ENGS = ("sync", "tensor", "vector", "scalar", "gpsimd")

    def __init__(self, nc, sems):
        self.nc = nc
        self.sems = sems
        self.plan = {e: [] for e in self.ENGS}
        self.emitted = {s: 0 for s in sems}
        self.hwm = {(e, s): 0 for e in self.ENGS for s in sems}

    def emit(self, eng, fn, waits=(), inc=None, inc_n=1):
        w = {}
        for tok in waits:
            if tok is None:
                continue
            s, v, peng = tok
            if peng == eng:
                continue
            assert v <= self.emitted[s], f"wait on future {s}:{v}>{self.emitted[s]}"
            if v > self.hwm[(eng, s)] and v > w.get(s, 0):
                w[s] = v
        for s, v in w.items():
            self.hwm[(eng, s)] = v
        tok = None
        if inc is not None:
            self.emitted[inc] += inc_n
            tok = (inc, self.emitted[inc], None if inc_n == 16 else eng)
        self.plan[eng].append((tuple(w.items()), fn, inc, inc_n))
        return tok

    def dma(self, eng, out, in_, waits=(), sem="dma", slow=False):
        if slow:
            return self.emit(eng, lambda e, o=out, i=in_: e.dma_start(out=o, in_=i,
                             allow_slow_non_contiguous=True), waits=waits, inc=sem, inc_n=16)
        return self.emit(eng, lambda e, o=out, i=in_: e.dma_start(out=o, in_=i),
                         waits=waits, inc=sem, inc_n=16)

    def simulate(self):
        import collections
        heads = {e: 0 for e in self.ENGS}
        sems = collections.defaultdict(int)
        progress = True
        while progress:
            progress = False
            for e in self.ENGS:
                while heads[e] < len(self.plan[e]):
                    waits, fn, inc, inc_n = self.plan[e][heads[e]]
                    if all(sems[s] >= v for s, v in waits):
                        if inc is not None:
                            sems[inc] += inc_n
                        heads[e] += 1
                        progress = True
                    else:
                        break
        stuck = {e: heads[e] for e in self.ENGS if heads[e] < len(self.plan[e])}
        if stuck:
            print("DEADLOCK detected; blocked heads:")
            for e, h in stuck.items():
                waits, fn, inc, inc_n = self.plan[e][h]
                print(f"  {e}[{h}/{len(self.plan[e])}] waits {waits} (sems: "
                      + ", ".join(f"{s}={sems[s]}" for s, v in waits) + ")")
            raise RuntimeError("deadlock")

    def replay(self, block):
        for ename in self.ENGS:
            section = getattr(block, ename)

            def make(ename):
                def _f(eng):
                    for waits, fn, inc, inc_n in self.plan[ename]:
                        for (s, v) in waits:
                            eng.wait_ge(self.sems[s], v)
                        if fn is None:
                            continue
                        ins = fn(eng)
                        if inc is not None:
                            if ename == "gpsimd" and inc_n != 16:
                                ins = eng.drain()
                            ins.then_inc(self.sems[inc], inc_n)
                return _f
            section(make(ename))


def build_core(debug=(), phases=2, nsup=NSUP, dbg_tile=None):
    nc = bass.Bass("TRN2", target_bir_lowering=False)
    dbg_t = {}

    xT_d = nc.dram_tensor("xT", [D, T], F32, kind="ExternalInput").ap()
    wqkv_d = nc.dram_tensor("wqkv", [D, 3 * D], F32, kind="ExternalInput").ap()
    wout_d = nc.dram_tensor("wout", [D, D], F32, kind="ExternalInput").ap()
    encw_d = nc.dram_tensor("encw", [D, EH], F32, kind="ExternalInput").ap()
    encb_d = nc.dram_tensor("encb", [EH], F32, kind="ExternalInput").ap()
    s2cw_d = nc.dram_tensor("s2cw", [EH, 64], F32, kind="ExternalInput").ap()
    s2cb_d = nc.dram_tensor("s2cb", [64], F32, kind="ExternalInput").ap()
    rw1_d = nc.dram_tensor("rw1", [64, 64], F32, kind="ExternalInput").ap()
    rb1_d = nc.dram_tensor("rb1", [64], F32, kind="ExternalInput").ap()
    rw2_d = nc.dram_tensor("rw2", [64, 8], F32, kind="ExternalInput").ap()
    rb2_d = nc.dram_tensor("rb2", [8], F32, kind="ExternalInput").ap()
    we1_d = nc.dram_tensor("we1", [8, 64, D], F32, kind="ExternalInput").ap()
    be1_d = nc.dram_tensor("be1", [8, D], F32, kind="ExternalInput").ap()
    we2_d = nc.dram_tensor("we2", [8, D, 64], F32, kind="ExternalInput").ap()
    be2_d = nc.dram_tensor("be2", [8, 64], F32, kind="ExternalInput").ap()
    c2sw_d = nc.dram_tensor("c2sw", [64, EH], F32, kind="ExternalInput").ap()
    c2sb_d = nc.dram_tensor("c2sb", [EH], F32, kind="ExternalInput").ap()
    decw_d = nc.dram_tensor("decw", [EH, D], F32, kind="ExternalInput").ap()
    decb_d = nc.dram_tensor("decb", [D], F32, kind="ExternalInput").ap()
    lng_d = nc.dram_tensor("lng", [D], F32, kind="ExternalInput").ap()
    lnb_d = nc.dram_tensor("lnb", [D], F32, kind="ExternalInput").ap()

    BF16 = mybir.dt.bfloat16
    ln1h_d = nc.dram_tensor("ln1h", [D, T], BF16, kind="Internal").ap()
    ln1l_d = nc.dram_tensor("ln1l", [D, T], BF16, kind="Internal").ap()
    wqh_d = nc.dram_tensor("wqh", [D, D], BF16, kind="Internal").ap()
    wql_d = nc.dram_tensor("wql", [D, D], BF16, kind="Internal").ap()
    wouth_d = nc.dram_tensor("wouth", [D, D], BF16, kind="Internal").ap()
    woutl_d = nc.dram_tensor("woutl", [D, D], BF16, kind="Internal").ap()
    ench_d = nc.dram_tensor("ench", [D, EH], BF16, kind="Internal").ap()
    encl_d = nc.dram_tensor("encl", [D, EH], BF16, kind="Internal").ap()
    outT_d = nc.dram_tensor("outT", [D, T], F32, kind="ExternalOutput").ap()

    def mkdbg(name, shape):
        if name in debug:
            dbg_t[name] = nc.dram_tensor("dbg_" + name, shape, F32, kind="ExternalOutput").ap()
            return dbg_t[name]
        return None

    dbg_ln1 = mkdbg("ln1T", [D, T])
    dbg_kv = mkdbg("kv_sb", [128, 1040])
    dbg_kf = mkdbg("kfe0", [128, 1024])
    dbg_vx = mkdbg("vext0", [128, 1040])
    dbg_kraw = mkdbg("kraw0", [128, 1024])
    dbg_bd = mkdbg("bd", [128, 128])
    dbg_qf = mkdbg("qfT", [D, T])
    dbg_wq = mkdbg("wqbuf", [128, 1024])
    dbg_whd = mkdbg("wqhd", [128, 512])
    dbg_qraw = mkdbg("qraw", [128, 1024])
    dbg_lnb = mkdbg("lnbuf", [128, 1024])
    dbg_den = mkdbg("invden", [16, T])
    dbg_att = mkdbg("attT", [D, T])
    dbg_y = mkdbg("yT", [D, T])
    dbg_ln2 = mkdbg("ln2T", [D, T])
    dbg_cont = mkdbg("contT", [64, T])
    dbg_spk = mkdbg("spk0", [128, 4096])
    dbg_encp = mkdbg("encp", [128, 512])
    dbg_lgtm = mkdbg("lgtm", [128, 32])
    dbg_tanh = mkdbg("tanh0", [64, 512])
    dbg_gd = mkdbg("gdbg", [128, 112])
    dbg_lgf = mkdbg("lgf", [8, 512])
    dbg_gtm2 = mkdbg("gtm2", [128, 32])
    dbg_ln2w = mkdbg("ln2w", [128, 512])
    dbg_encw = mkdbg("encwb", [128, 2048])
    dbg_gat = mkdbg("gatesT", [8, T])
    dbg_acc = mkdbg("accT", [64, T])
    dbg_dec = mkdbg("decT", [D, T])

    from contextlib import ExitStack
    es = ExitStack()

    def SB(name, p, cols, dt=F32):
        return es.enter_context(nc.sbuf_tensor("sb_" + name, [p, cols], dt))

    with es:
        ident = SB("ident", 128, 128)
        onescol = SB("onescol", 128, 1)
        psel16 = SB("psel16", 128, 1024)
        invf_col = SB("invf_col", 128, 1)
        sgn_col = SB("sgn_col", 128, 1)
        invf_rep = SB("invf_rep", 128, 64)
        iotap = SB("iotap", 128, 1)
        postm = SB("postm", 128, 1)
        kv_sb = SB("kv_sb", 128, 1040)
        bd = SB("bd", 128, 128)
        rowp = SB("rowp", 128, 1536)
        C_ST, C_RT, C_ID = 0, 512, 1024
        s2cw = SB("s2cw", 128, 1024)
        rw1 = SB("rw1", 64, 64)
        rw2t = SB("rw2t", 128, 8)
        be1c = SB("be1c", 128, 64)
        be2 = SB("be2", 8, 64)
        encb = SB("encb", 128, 16)
        decb = SB("decb", 128, 8)
        s2cb = SB("s2cb", 64, 1)
        rb1t = SB("rb1t", 128, 1)
        rb2 = SB("rb2", 8, 1)
        c2sb = SB("c2sb", 128, 16)
        lng = SB("lng", 128, 8)
        lnb = SB("lnb", 128, 8)
        cosrep = SB("cosrep", 128, 512)
        sinB = SB("sinB", 128, 512)
        rs1 = SB("rs1", 128, 512)
        rs2 = SB("rs2", 128, 512)
        rs3 = SB("rs3", 128, 512)
        costm = SB("costm", 128, 64)
        sintm = SB("sintm", 128, 64)
        rtm1 = SB("rtm1", 128, 64)
        rtm2 = SB("rtm2", 128, 64)
        th1 = SB("th1", 128, 512)
        th2 = SB("th2", 128, 512)
        pos0 = SB("pos0", 128, 512)
        invft = SB("invft", 128, 512)
        abc = SB("abc", 128, 512)
        bbc = SB("bbc", 128, 512)
        AR1 = SB("AR1", 128, 16384)
        AR2 = SB("AR2", 128, 12288)
        AR3 = SB("AR3", 128, 8192)
        AR5 = SB("AR5", 128, 4096)
        contT = SB("contT", 64, 512)
        accT = SB("accT", 64, 512)
        gtm = SB("gtm", 128, 32)

        pbt = [es.enter_context(nc.psum_tensor(f"pb{i}", [128, 512], F32)) for i in range(8)]
        pb = [t.ap() for t in pbt]

        sems = {}
        for sname in ("dma", "dst", "T", "V", "S", "G",
                      "Lln", "LxT", "Lwq0", "Lwq1", "Lwo0", "Lwo1",
                      "Len0", "Len1", "Ldc0", "Ldc1",
                      "Lw10", "Lw11", "Lw20", "Lw21",
                      "Lc2", "Lsm", "Lsp", "Dsp", "Ds2", "Sln",
                      "LxA", "LlA", "LkA"):
            sems[sname] = es.enter_context(nc.semaphore(sname))
        block = es.enter_context(nc.Block())
        p = Prog(nc, sems)
        p.dbg_theta = None
        p.dbg_theta_tok = None

        wkv = AR1.ap()
        a_xT = AR2[:, 0:4096]
        a_ln1 = AR2[:, 4096:8192]
        k_raw = AR3[:, 0:1024]
        ks1 = AR3[:, 1024:2048]
        ks2 = AR3[:, 2048:3072]
        _kf16 = AR3[:, 3072:4096].bitcast(mybir.dt.bfloat16)
        kfh = _kf16[:, 0:1024]
        kfl = _kf16[:, 1024:2048]
        _vx16 = AR3[:, 4096:5136].bitcast(mybir.dt.bfloat16)
        vxh = _vx16[:, 0:1040]
        vxl = _vx16[:, 1040:2080]
        b_enc = [AR1[:, 0:2048], AR1[:, 2048:4096]]
        b_dec = [AR1[:, 4096:5120], AR1[:, 5120:6144]]
        b_wq = [AR1[:, 6144:7168], AR1[:, 7168:8192]]
        b_wo = [AR1[:, 8192:9216], AR1[:, 9216:10240]]
        b_we1 = [AR1[0:64, 10240:11264], AR1[0:64, 11264:12288]]
        b_we2 = [AR1[:, 12288:12800], AR1[:, 12800:13312]]
        b_c2s = AR1[0:64, 13312:15360]
        b_ln1 = AR2[:, 0:4096]
        b_ln2 = AR2[:, 0:4096]
        _ln16 = AR2[:, 0:4096].bitcast(mybir.dt.bfloat16)
        lnh = _ln16[:, 0:4096]
        lnl = _ln16[:, 4096:8192]
        _a16 = AR2[:, 4096:8192].bitcast(mybir.dt.bfloat16)
        a_ln1h = _a16[:, 0:4096]
        a_ln1l = _a16[:, 4096:8192]
        _att16 = AR3[:, 0:4096].bitcast(mybir.dt.bfloat16)
        atth = _att16[:, 0:4096]
        attl = _att16[:, 4096:8192]
        b_xT = AR2[:, 4096:8192]
        b_he = AR2[:, 4096:8192]
        b_qf = AR2[:, 8192:12288]
        b_decT = AR2[:, 8192:12288]
        b_att = AR3[:, 0:4096]
        b_outT = AR3[:, 0:4096]
        b_yT = AR3[:, 4096:8192]
        b_spk = AR5.ap()
        b_moe = AR5.ap()
        b_moe1 = AR3[:, 0:4096]

        pfree = {i: None for i in range(8)}
        fr = {}

        t0 = p.emit("vector", lambda e: e.memset(ident[:], 1.0), inc="V")
        tok_ident = p.emit("gpsimd", lambda e: e.affine_select(
            ident[:], ident[:], pattern=[[-1, 128]], compare_op=OP.is_equal,
            fill=0.0, base=0, channel_multiplier=1), waits=[t0], inc="G")
        p.emit("vector", lambda e: e.memset(onescol[:], 1.0))
        rowi = rowp[0:1, 0:512]
        t_r = None
        for c in range(128):
            t_r = p.emit("vector", lambda e, c=c: e.memset(rowp[:, c:c + 1], float(c)),
                         inc="V" if c == 127 else None)
        tok_trio = p.emit("tensor", lambda e: e.transpose(pb[0][:, 0:1], rowi[0:1, 0:128],
                                                          ident[0:1, 0:1]),
                          waits=[t_r, tok_ident], inc="T")
        tok_iota = p.emit("vector", lambda e: e.tensor_copy(iotap[:], pb[0][:, 0:1]),
                          waits=[tok_trio], inc="V")
        if dbg_whd is not None:
            tsd = p.emit("vector", lambda e: e.tensor_copy(rs3[:, 0:4], pb[0][:, 0:4]),
                         waits=[tok_trio], inc="V")
            fr["tsd"] = tsd
            p.dma("gpsimd", dbg_whd[:, 0:4], rs3[:, 0:4], waits=[tsd], sem="dst", slow=True)
            p.dma("gpsimd", dbg_whd[0:1, 4:132], rowi[0:1, 0:128], waits=[tok_trio], sem="dst")
            p.dma("gpsimd", dbg_whd[:, 132:133], iotap[:], waits=[tok_iota], sem="dst", slow=True)
            fr["whd_dbg"] = p.dma("gpsimd", dbg_whd[:, 133:261], ident[:], waits=[tok_ident],
                                  sem="dst")
        for v8 in range(8):
            p.emit("vector", lambda e, v8=v8: e.memset(rs1[:, 64 * v8:64 * (v8 + 1)], float(v8)))
            t_r = p.emit("vector", lambda e, v8=v8: e.memset(rs2[:, 64 * v8:64 * (v8 + 1)],
                                                             float(v8 + 8.0)),
                         inc="V" if v8 == 7 else None)
        p.emit("vector", lambda e: e.tensor_scalar(
            psel16[:, 0:512], rs1[:], iotap[:], None, op0=OP.is_equal),
            waits=[tok_iota])
        tok_psel = p.emit("vector", lambda e: e.tensor_scalar(
            psel16[:, 512:1024], rs2[:], iotap[:], None, op0=OP.is_equal), inc="V")
        fr["srow"] = tok_psel
        fr["lnsq0"] = tok_psel
        fr["lnsq1"] = tok_psel
        pfree[0] = tok_iota
        t_r2 = None
        for c in range(128):
            iv = float(np.float32(ROPE_BASE ** (-(c % 32) / 32.0)))
            sv = -1.0 if (c % 64) < 32 else 1.0
            p.emit("vector", lambda e, c=c, iv=iv: e.memset(rowp[:, 128 + c:129 + c], iv))
            t_r2 = p.emit("vector", lambda e, c=c, sv=sv: e.memset(rowp[:, 256 + c:257 + c], sv),
                          inc="V" if c == 127 else None)
        tok_ti2 = p.emit("tensor", lambda e: e.transpose(pb[0][:, 1:2], rowp[0:1, 128:256],
                                                         ident[0:1, 0:1]),
                         waits=[t_r2, tok_iota, fr.get("tsd")], inc="T")
        tok_invc = p.emit("vector", lambda e: e.tensor_copy(invf_col[:], pb[0][:, 1:2]),
                          waits=[tok_ti2], inc="V")
        tok_ti3 = p.emit("tensor", lambda e: e.transpose(pb[0][:, 2:3], rowp[0:1, 256:384],
                                                         ident[0:1, 0:1]), waits=[tok_invc], inc="T")
        tok_sgn = p.emit("vector", lambda e: e.tensor_copy(sgn_col[:], pb[0][:, 2:3]),
                         waits=[tok_ti3], inc="V")
        pfree[0] = tok_sgn
        for c in range(512):
            t_p0 = p.emit("vector", lambda e, c=c: e.memset(pos0[:, c:c + 1], float(c)),
                          inc="V" if c == 511 else None)
        t_o1 = p.emit("vector", lambda e: e.memset(rs3[:], 1.0), inc="V")
        tok_ivm = p.emit("tensor", lambda e: e.matmul(pb[2][:], rowp[0:1, 128:256],
                                                      rs3[0:1, 0:512], start=True, stop=True),
                         waits=[t_o1, t_r2, tok_psel, pfree[2]], inc="T")
        tok_ivt = p.emit("vector", lambda e: e.tensor_copy(invft[:], pb[2][:]),
                         waits=[tok_ivm], inc="V")
        fr["srow"] = tok_ivm
        pfree[2] = tok_ivt
        for j in range(32):
            iv = float(np.float32(ROPE_BASE ** (-j / 32.0)))
            p.emit("vector", lambda e, j=j, iv=iv: e.memset(invf_rep[:, j:j + 1], iv))
            tok_invr = p.emit("vector", lambda e, j=j, iv=iv: e.memset(
                invf_rep[:, 32 + j:33 + j], iv), inc="V" if j == 31 else None)

        if dbg_lnb is not None:
            tcp0 = p.emit("vector", lambda e: e.tensor_copy(costm[:, 0:1], sgn_col[:]),
                          waits=[tok_sgn, tok_invc], inc="V")
            tcp0b = p.emit("vector", lambda e: e.tensor_copy(costm[:, 1:2], invf_col[:]), inc="V")
            fr_dbgA = p.dma("gpsimd", dbg_lnb[:, 0:2], costm[:, 0:2], waits=[tcp0b], sem="dst")
        p.dma("scalar", s2cw.ap().rearrange("p (c m) -> p c m", m=64), s2cw_d.rearrange("(c p) m -> p c m", p=128), sem="Lsm")
        p.dma("scalar", rw1[:], rw1_d, sem="Lsm")
        p.dma("scalar", rw2t[64:128, :], rw2_d, sem="Lsm")
        p.dma("scalar", be1c.ap().rearrange("p (e c) -> p e c", c=8), be1_d.rearrange("e (c p) -> p e c", p=128), slow=True, sem="Lsm")
        p.dma("scalar", be2[:], be2_d, sem="Lsm")
        p.dma("scalar", encb[:], encb_d.rearrange("(c p) -> p c", p=128), slow=True, sem="Lsm")
        p.dma("scalar", decb[:], decb_d.rearrange("(c p) -> p c", p=128), slow=True, sem="Lsm")
        p.dma("scalar", s2cb[:], s2cb_d.rearrange("(c p) -> p c", p=64), slow=True, sem="Lsm")
        p.dma("scalar", rb1t[64:128, :], rb1_d.rearrange("(c p) -> p c", p=64), slow=True, sem="Lsm")
        p.dma("scalar", rb2[:], rb2_d.rearrange("(c p) -> p c", p=8), slow=True, sem="Lsm")
        p.dma("scalar", c2sb[:], c2sb_d.rearrange("(c p) -> p c", p=128), slow=True, sem="Lsm")
        p.dma("scalar", lng[:], lng_d.rearrange("(c p) -> p c", p=128), slow=True, sem="Lsm")
        tok_wsmall = p.dma("scalar", lnb[:], lnb_d.rearrange("(c p) -> p c", p=128), slow=True, sem="Lsm")
        tok_c2sw = p.dma("scalar", b_c2s, c2sw_d, sem="Lc2")
        wkv16 = AR1.ap().bitcast(mybir.dt.bfloat16)
        wkvh = wkv16[:, 0:16384]
        wkvl = wkv16[:, 16384:32768]
        slot_f = [AR5[:, 0:1024], AR5[:, 2048:3072]]
        slot_16 = [AR5[:, 1024:2048].bitcast(mybir.dt.bfloat16),
                   AR5[:, 3072:4096].bitcast(mybir.dt.bfloat16)]
        slot_busy = [None, None]
        slot_i = [0]

        def emit_split(src_ap, hi_dst=None, lo_dst=None, sbuf_hi=None, sbuf_lo=None):
            i = slot_i[0] & 1
            slot_i[0] += 1
            stgf, stg16 = slot_f[i], slot_16[i]
            stgh, stgl = stg16[:, 0:1024], stg16[:, 1024:2048]
            t_ld = p.dma("sync", stgf[:], src_ap, waits=[slot_busy[i]], sem="Lsp")
            p.emit("vector", lambda e: e.tensor_copy(stgh[:], stgf[:]), waits=[t_ld])
            t_lo = p.emit("vector", lambda e: e.tensor_tensor(
                stgl[:], stgf[:], stgh[:], op=OP.subtract), inc="V")
            toks = []
            if hi_dst is not None:
                ds = "Dsp" if i == 0 else "Ds2"
                toks.append(p.dma("gpsimd", hi_dst, stgh[:], waits=[t_lo], sem=ds))
                toks.append(p.dma("gpsimd", lo_dst, stgl[:], waits=[t_lo], sem=ds))
            if sbuf_hi is not None:
                p.emit("vector", lambda e: e.tensor_copy(sbuf_hi, stgh[:]), waits=[t_lo])
                toks.append(p.emit("vector", lambda e: e.tensor_copy(sbuf_lo, stgl[:]), inc="V"))
            slot_busy[i] = toks[-1]
            return toks[-1]

        def ln_fm(chunks, st, b1, b2):
            n = len(chunks)
            sqbuf = [rs1, rs2]
            sq_gate = [fr.get("lnsq0"), fr.get("lnsq1")]
            last_mm = None
            for k, (ch, rdy) in enumerate(chunks):
                sq = sqbuf[k % 2]
                tok_sq = p.emit("scalar", lambda e, c=ch, o=sq: e.activation(o[:], c, AF.Square),
                                waits=[rdy, sq_gate[k % 2], fr.get("rope_dbg")], inc="S")
                p.emit("tensor", lambda e, c=ch, k=k, n=n, st=st: e.matmul(
                    pb[st][0:1, :], onescol[:], c, start=(k == 0), stop=(k == n - 1)),
                    waits=[rdy] + ([pfree[st]] if k == 0 else []))
                last_mm = p.emit("tensor", lambda e, o=sq, k=k, n=n, st=st: e.matmul(
                    pb[st][32:33, :], onescol[:], o[:], start=(k == 0), stop=(k == n - 1)),
                    waits=[tok_sq], inc="T")
                sq_gate[k % 2] = last_mm
            fr["sqmm"] = last_mm
            p.emit("vector", lambda e: e.tensor_copy(rowp[0:1, C_ST:C_ST + 512], pb[st][0:1, :]),
                   waits=[last_mm, fr.get("srow")])
            tok_sr = p.emit("vector", lambda e: e.tensor_copy(rowp[32:33, C_ST:C_ST + 512], pb[st][32:33, :]),
                            inc="V")
            pfree[st] = tok_sr
            tb1 = p.emit("tensor", lambda e, b1=b1: e.matmul(
                pb[b1][:], invft[0:1, 0:128], rowp[0:1, C_ST:C_ST + 512], start=True, stop=True),
                waits=[tok_sr, pfree[b1]], inc="T")
            tb2 = p.emit("tensor", lambda e, b2=b2: e.matmul(
                pb[b2][:], invft[32:33, 0:128], rowp[32:33, C_ST:C_ST + 512], start=True, stop=True),
                waits=[tok_sr, pfree[b2]], inc="T")
            fr["srow"] = tb2
            inv_n = 1.0 / (128 * n)
            t_m = p.emit("vector", lambda e, b1=b1: e.tensor_scalar(rs1[:], pb[b1][:], inv_n, None, op0=OP.mult),
                         waits=[tb1, tok_invc, tok_invr], inc="V")
            t_q = p.emit("vector", lambda e, b2=b2: e.tensor_scalar(rs2[:], pb[b2][:], inv_n, LNEPS,
                                                                     op0=OP.mult, op1=OP.add),
                         waits=[tb2], inc="V")
            fr["lnsq0"], fr["lnsq1"] = t_m, t_q
            pfree[b1] = t_m
            pfree[b2] = t_q
            p.emit("vector", lambda e: e.tensor_tensor(abc[:], rs1[:], rs1[:], op=OP.mult),
                   waits=[fr.get("abcS")])
            p.emit("vector", lambda e: e.tensor_tensor(abc[:], rs2[:], abc[:], op=OP.subtract))
            tok_var = p.emit("vector", lambda e: e.reciprocal(bbc[:], abc[:]), inc="V")
            tok_a = p.emit("scalar", lambda e: e.activation(abc[:], bbc[:], AF.Sqrt),
                           waits=[tok_var], inc="S")
            fr["abcS"] = tok_a
            tok_b = p.emit("vector", lambda e: e.scalar_tensor_tensor(
                bbc[:], rs1[:], -1.0, abc[:], op0=OP.mult, op1=OP.mult), waits=[tok_a], inc="V")
            return tok_a, tok_b

        def rope_sincos(theta, sc_i, sc_f, out_sin, out_cos, first_waits=(), sin_waits=(), cos_waits=()):
            p.emit("vector", lambda e: e.tensor_scalar(sc_f, theta, 1.0 / TWO_PI, None, op0=OP.mult),
                   waits=list(first_waits))
            p.emit("vector", lambda e: e.tensor_copy(sc_i, sc_f))
            p.emit("vector", lambda e: e.tensor_copy(sc_f, sc_i))
            p.emit("vector", lambda e: e.scalar_tensor_tensor(theta, sc_f, -CW1, theta, op0=OP.mult, op1=OP.add))
            p.emit("vector", lambda e: e.scalar_tensor_tensor(theta, sc_f, -CW2, theta, op0=OP.mult, op1=OP.add))
            p.emit("vector", lambda e: e.tensor_scalar(sc_f, theta, -PI, None, op0=OP.is_lt))
            p.emit("vector", lambda e: e.tensor_scalar(out_sin, theta, PI, None, op0=OP.is_gt))
            p.emit("vector", lambda e: e.tensor_tensor(sc_f, sc_f, out_sin, op=OP.subtract))
            tok_th = p.emit("vector", lambda e: e.scalar_tensor_tensor(
                theta, sc_f, TWO_PI, theta, op0=OP.mult, op1=OP.add), inc="V")
            if p.dbg_theta is not None:
                p.dbg_theta_tok = p.dma("gpsimd", p.dbg_theta, theta, waits=[tok_th], sem="dst")
                p.dbg_theta = None
            tok_s = p.emit("scalar", lambda e: e.activation(out_sin, theta, AF.Sin),
                           waits=[tok_th, p.dbg_theta_tok] + list(sin_waits), inc="S")
            tok_abs = p.emit("scalar", lambda e: e.activation(sc_f, theta, AF.Abs), inc="S")
            fr["theta_abs"] = tok_abs
            tok_neg = p.emit("vector", lambda e: e.tensor_scalar(sc_f, sc_f, -1.0, PI / 2,
                                                                 op0=OP.mult, op1=OP.add),
                             waits=[tok_abs], inc="V")
            tok_c = p.emit("scalar", lambda e: e.activation(out_cos, sc_f, AF.Sin),
                           waits=[tok_neg] + list(cos_waits), inc="S")
            return tok_s, tok_c

        KVB = (5, 6, 7)
        xT_free = []
        kbuf_free = None
        first_kv = [True, True, True]
        last_kv_mm = None
        p.emit("vector", lambda e: e.memset(vxh[:], 1.0))
        tok_vext_ones = p.emit("vector", lambda e: e.memset(vxl[:], 0.0), inc="V")

        _a16b = AR2[:, 8192:12288].bitcast(mybir.dt.bfloat16)
        a_ln1X = [(a_ln1h, a_ln1l), (_a16b[:, 0:4096], _a16b[:, 4096:8192])]
        ln1_freeP = {0: [], 1: []}
        ln1_store = {}

        def emit_ln1_stats(s):
            if s == 0:
                xt = xt0_toks
            else:
                xt = []
                for k in range(NK):
                    xt.append(p.dma("sync", a_xT[:, 512 * k:512 * (k + 1)],
                                    xT_d[128 * k:128 * (k + 1), SUP * s:SUP * (s + 1)],
                                    waits=xT_free if k == 0 else (), sem="LxA"))
            chunks = [(a_xT[:, 512 * k:512 * (k + 1)], xt[k]) for k in range(NK)]
            tok_a, tok_b = ln_fm(chunks, 2, 3, 4)
            return xt, tok_a, tok_b

        def emit_ln1_apply(s, xt, tok_a, tok_b):
            nonlocal xT_free
            hT, lT = a_ln1X[s % 2]
            apply_toks = []
            for k in range(NK):
                p.emit("vector", lambda e, k=k: e.tensor_tensor(
                    rs3[:], a_xT[:, 512 * k:512 * (k + 1)], abc[:], op=OP.mult),
                    waits=([tok_a] + ln1_freeP[s % 2]) if k == 0 else [])
                t_ln = p.emit("vector", lambda e, k=k: e.tensor_tensor(rs3[:], rs3[:], bbc[:], op=OP.add),
                              waits=[tok_b] if k == 0 else [], inc="V")
                t_h = p.emit("vector", lambda e, k=k, hT=hT: e.tensor_copy(
                    hT[:, 512 * k:512 * (k + 1)], rs3[:]), waits=[t_ln], inc="V")
                apply_toks.append(p.emit("vector", lambda e, k=k, hT=hT, lT=lT: e.tensor_tensor(
                    lT[:, 512 * k:512 * (k + 1)], rs3[:], hT[:, 512 * k:512 * (k + 1)],
                    op=OP.subtract), waits=[t_h], inc="V"))
            st_toks = []
            for k in range(NK):
                st_toks.append(p.dma("gpsimd", ln1h_d[128 * k:128 * (k + 1), SUP * s:SUP * (s + 1)],
                                     hT[:, 512 * k:512 * (k + 1)], waits=[apply_toks[k]], sem="Sln"))
                st_toks.append(p.dma("gpsimd", ln1l_d[128 * k:128 * (k + 1), SUP * s:SUP * (s + 1)],
                                     lT[:, 512 * k:512 * (k + 1)], waits=[apply_toks[k]], sem="Sln"))
            ln1_store[s] = st_toks[-1]
            xT_free = [apply_toks[-1]]
            return apply_toks, st_toks

        xt0_toks = []
        for k in range(NK):
            xt0_toks.append(p.dma("sync", a_xT[:, 512 * k:512 * (k + 1)],
                                  xT_d[128 * k:128 * (k + 1), 0:SUP], sem="LxA"))
        st0 = emit_ln1_stats(0)
        apply_cur, st_cur = emit_ln1_apply(0, *st0)

        wkv_toks = []
        for k in range(NK):
            emit_split(wqkv_d[128 * k:128 * (k + 1), 1024:2048],
                       sbuf_hi=wkvh[:, k * 2048:k * 2048 + 1024],
                       sbuf_lo=wkvl[:, k * 2048:k * 2048 + 1024])
            wkv_toks.append(emit_split(wqkv_d[128 * k:128 * (k + 1), 2048:3072],
                       sbuf_hi=wkvh[:, k * 2048 + 1024:(k + 1) * 2048],
                       sbuf_lo=wkvl[:, k * 2048 + 1024:(k + 1) * 2048]))
        for k in range(NK):
            emit_split(wqkv_d[128 * k:128 * (k + 1), 0:1024],
                       hi_dst=wqh_d[128 * k:128 * (k + 1), :],
                       lo_dst=wql_d[128 * k:128 * (k + 1), :])
        for k in range(NK):
            emit_split(wout_d[128 * k:128 * (k + 1), :],
                       hi_dst=wouth_d[128 * k:128 * (k + 1), :],
                       lo_dst=woutl_d[128 * k:128 * (k + 1), :])
        tok_wsplit = None
        for k in range(NK):
            for h in range(2):
                tok_splitB_lo = tok_wsplit
                tok_wsplit = emit_split(encw_d[128 * k:128 * (k + 1), 1024 * h:1024 * (h + 1)],
                                        hi_dst=ench_d[128 * k:128 * (k + 1), 1024 * h:1024 * (h + 1)],
                                        lo_dst=encl_d[128 * k:128 * (k + 1), 1024 * h:1024 * (h + 1)])
        if dbg_lnb is not None:
            tcp1 = p.emit("vector", lambda e: e.tensor_copy(costm[:, 2:3], sgn_col[:]),
                          waits=[tok_wsplit, tok_splitB_lo, fr_dbgA], inc="V")
            p.dma("gpsimd", dbg_lnb[:, 2:4], costm[:, 2:4], waits=[tcp1], sem="dst")


        for s in range(nsup):
            apply_toks = apply_cur
            a_ln1h_c, a_ln1l_c = a_ln1X[s % 2]
            for u in range(NSUB):
                if u == 1 and s + 1 < nsup:
                    stn = emit_ln1_stats(s + 1)
                if u == 2 and s + 1 < nsup:
                    apply_nxt, st_nxt = emit_ln1_apply(s + 1, *stn)
                tile = 4 * s + u
                evac_toks = []
                for j in range(4):
                    bank = j % 2
                    mm_tok = None
                    for k in range(NK):
                        for ps_ in range(3):
                            la, rb = [(a_ln1h_c, wkvh), (a_ln1h_c, wkvl), (a_ln1l_c, wkvh)][ps_]
                            mm_tok = p.emit("tensor", lambda e, j=j, k=k, u=u, bank=bank, la=la, rb=rb,
                                            st_=(k == 0 and ps_ == 0),
                                            sp=(k == NK - 1 and ps_ == 2): e.matmul(
                                pb[bank][:], la[:, 512 * k + 128 * u:512 * k + 128 * u + 128],
                                rb[:, 2048 * k + 512 * j:2048 * k + 512 * (j + 1)],
                                start=st_, stop=sp),
                                waits=([apply_toks[k], pfree[bank], wkv_toks[k]] if (k == 0 and ps_ == 0)
                                       else ([apply_toks[k], wkv_toks[k]] if ps_ == 0 else [])),
                                inc="T" if (k == NK - 1 and ps_ == 2) else None)
                    if j < 2:
                        etok = p.emit("vector", lambda e, j=j, bank=bank: e.tensor_copy(
                            k_raw[:, 512 * j:512 * (j + 1)], pb[bank][:]),
                            waits=[mm_tok, kbuf_free], inc="V")
                    else:
                        jj = j - 2
                        vh_view = vxh.rearrange("p (sl c) -> p sl c", c=130)[:, 4 * jj:4 * jj + 4, 0:128]
                        vl_view = vxl.rearrange("p (sl c) -> p sl c", c=130)[:, 4 * jj:4 * jj + 4, 0:128]
                        ps_view = pb[bank].rearrange("p (sl c) -> p sl c", c=128)
                        p.emit("vector", lambda e, o=vh_view, i=ps_view: e.tensor_copy(o, i),
                               waits=[mm_tok, kbuf_free, tok_vext_ones])
                        etok = p.emit("vector", lambda e, o=vl_view, i=ps_view, h=vh_view: e.tensor_tensor(
                            o, i, h, op=OP.subtract), inc="V")
                    evac_toks.append(etok)
                    pfree[bank] = etok
                tok_pi = p.emit("gpsimd", lambda e, tile=tile: e.iota(
                    postm[:], pattern=[[0, 1]], base=128 * tile, channel_multiplier=1,
                    allow_small_or_imprecise_dtypes=True),
                    waits=[tok_invr, fr.get("postm")], inc="G")
                tok_th = p.emit("vector", lambda e: e.tensor_scalar(
                    rtm1[:], invf_rep[:], postm[:], None, op0=OP.mult),
                    waits=[tok_pi, tok_invr, fr.get("tmS"), fr.get("tmC")], inc="V")
                fr["postm"] = tok_th
                ts_tok, tc_tok = rope_sincos(rtm1[:], costm.ap().bitcast(I32), rtm2[:], sintm[:], costm[:],
                                             first_waits=[tok_th])
                tsgn = p.emit("vector", lambda e: e.tensor_scalar(
                    sintm[:, 0:32], sintm[:, 0:32], -1.0, None, op0=OP.mult), waits=[ts_tok], inc="V")
                fr["tmS"] = tsgn
                fr["tmC"] = tc_tok
                k4 = k_raw.rearrange("p (h q j) -> p h q j", q=2, j=32)
                s4 = ks1.rearrange("p (h q j) -> p h q j", q=2, j=32)
                p.emit("vector", lambda e: e.tensor_copy(s4[:, :, 0, :], k4[:, :, 1, :]),
                       waits=[evac_toks[1]])
                tok_ksw = p.emit("vector", lambda e: e.tensor_copy(s4[:, :, 1, :], k4[:, :, 0, :]), inc="V")
                cos_b = costm.ap().unsqueeze(1).to_broadcast([128, 16, 64])
                sin_b = sintm.ap().unsqueeze(1).to_broadcast([128, 16, 64])
                p.emit("vector", lambda e: e.tensor_tensor(
                    ks2.rearrange("p (h j) -> p h j", j=64),
                    k_raw.rearrange("p (h j) -> p h j", j=64), cos_b, op=OP.mult),
                    waits=[tc_tok, evac_toks[1]])
                p.emit("vector", lambda e: e.tensor_tensor(
                    s4.rearrange("p h q j -> p (h q j)"),
                    ks1.rearrange("p (h j) -> p h j", j=64), sin_b, op=OP.mult),
                    waits=[tsgn, tok_ksw])
                tok_kr = p.emit("vector", lambda e: e.tensor_tensor(ks2[:], ks2[:], ks1[:], op=OP.add),
                                inc="V")
                tok_mn = p.emit("vector", lambda e: e.tensor_scalar(ks1[:], ks2[:], 0.0, None, op0=OP.min),
                                inc="V")
                tok_ex = p.emit("scalar", lambda e: e.activation(ks1[:], ks1[:], AF.Exp),
                                waits=[tok_mn], inc="S")
                p.emit("vector", lambda e: e.scalar_tensor_tensor(
                    k_raw[:], ks2[:], 0.0, ks1[:], op0=OP.max, op1=OP.add), waits=[tok_ex])
                p.emit("vector", lambda e: e.tensor_copy(kfh[:], k_raw[:]))
                tok_kf = p.emit("vector", lambda e: e.tensor_tensor(
                    kfl[:], k_raw[:], kfh[:], op=OP.subtract), inc="V")
                for pr in range(8):
                    bank = KVB[pr // 3]
                    off = 129 * (pr % 3)
                    for ps_ in range(3):
                        ka, vb = [(kfh, vxh), (kfh, vxl), (kfl, vxh)][ps_]
                        st_ = first_kv[pr // 3] and ps_ == 0
                        if ps_ == 0:
                            first_kv[pr // 3] = False
                        last = (tile == 4 * nsup - 1) and (pr == 7) and (ps_ == 2)
                        last_kv_mm = p.emit("tensor", lambda e, pr=pr, bank=bank, off=off, st_=st_,
                                            last=last, ka=ka, vb=vb: e.matmul(
                            pb[bank][:, off:off + 129], ka[:, 128 * pr:128 * (pr + 1)],
                            vb[:, 130 * pr:130 * pr + 129], start=st_, stop=last),
                            waits=[tok_kf, evac_toks[3]] if (pr == 0 and ps_ == 0) else [],
                            inc="T" if (pr == 7 and ps_ == 2) else None)
                if tile == (dbg_tile if dbg_tile is not None else 4 * nsup - 1):
                    if dbg_kf is not None:
                        p.dma("gpsimd", dbg_kf, k_raw[:], waits=[tok_kf], sem="dst")
                    if dbg_vx is not None:
                        p.dma("gpsimd", dbg_vx, vxh[:], waits=[evac_toks[3]], sem="dst")
                    if dbg_kraw is not None:
                        p.dma("gpsimd", dbg_kraw, k_raw[:], waits=[evac_toks[1]], sem="dst")
                kbuf_free = last_kv_mm
            ln1_freeP[s % 2] = [last_kv_mm, st_cur[-1]]
            if s + 1 < nsup:
                apply_cur, st_cur = apply_nxt, st_nxt

        kv_evac = []
        for b in range(3):
            nslot = 3 if b < 2 else 2
            kv_evac.append(p.emit("vector", lambda e, b=b, nslot=nslot: e.tensor_copy(
                kv_sb.rearrange("p (sl c) -> p sl c", c=130)[:, 3 * b:3 * b + nslot, 0:129],
                pb[KVB[b]][:, 0:129 * nslot].rearrange("p (sl c) -> p sl c", c=129)),
                waits=[last_kv_mm], inc="V"))
        for b in range(3):
            pfree[KVB[b]] = kv_evac[b]
        tok_bd0 = p.emit("vector", lambda e: e.memset(bd[:], 0.0), inc="V")
        tok_bd = None
        for c in range(NK):
            p.emit("vector", lambda e, c=c: e.tensor_copy(
                bd[0:64, 18 * c:18 * c + 1], kv_sb[0:64, 130 * c + 128:130 * c + 129]),
                waits=[kv_evac[c // 3], tok_bd0])
            tok_bd = p.emit("vector", lambda e, c=c: e.tensor_copy(
                bd[64:128, 18 * c + 1:18 * c + 2], kv_sb[64:128, 130 * c + 128:130 * c + 129]),
                inc="V")
        if dbg_kv is not None:
            p.dma("gpsimd", dbg_kv, kv_sb[:], waits=[kv_evac[-1]], sem="dst")
        if dbg_bd is not None:
            p.dma("gpsimd", dbg_bd, bd[:], waits=[tok_bd], sem="dst")

        if dbg_lnb is not None:
            tcp2 = p.emit("vector", lambda e: e.tensor_copy(costm[:, 4:5], sgn_col[:]),
                          waits=[last_kv_mm], inc="V")
            p.dma("gpsimd", dbg_lnb[:, 4:6], costm[:, 4:6], waits=[tcp2], sem="dst")
        out_store_toks = [last_kv_mm]

        if phases >= 2:
            fr.update({
                "b_ln1": last_kv_mm, "b_xT": last_kv_mm, "b_he": last_kv_mm,
                "b_qf": last_kv_mm, "b_att": last_kv_mm, "b_yT": last_kv_mm,
                "b_spk": last_kv_mm, "contT": None, "accT": None, "gates": None,
                "invden": None, "tanh": None, "logit": None,
            })
            for kk in ("b_enc0", "b_enc1", "b_dec0", "b_dec1", "b_wq0", "b_wq1",
                       "b_wo0", "b_wo1", "b_we10", "b_we11", "b_we20", "b_we21"):
                fr[kk] = last_kv_mm
            out_store_toks = []
            mm2 = None

            def emit_rope(s2):
                tok_th = p.emit("vector", lambda e, s2=s2: e.scalar_tensor_tensor(
                    th1[:], pos0[:], float(SUP * s2), invft[:], op0=OP.add, op1=OP.mult),
                    waits=[fr.get("th1F"), tok_ivt, fr.get("fmC")]
                          + ([last_kv_mm] if s2 == 0 else []), inc="V")
                ci_b = cosrep.ap().bitcast(I32)
                ts_, tc_ = rope_sincos(th1[:], ci_b, th2[:], sinB[:], cosrep[:],
                                       first_waits=[tok_th])
                fr["th1F"] = fr.get("theta_abs")
                sg2 = p.emit("vector", lambda e: e.tensor_scalar(
                    sinB[:], sinB[:], sgn_col[:], None, op0=OP.mult), waits=[ts_, tok_sgn], inc="V")
                fr["fmC"] = tc_
                return ts_, tc_, sg2

            rope_next = emit_rope(0)

            def emit_ln1_loads(s2):
                toks = []
                for k in range(NK):
                    p.dma("sync", lnh[:, 512 * k:512 * (k + 1)],
                          ln1h_d[128 * k:128 * (k + 1), SUP * s2:SUP * (s2 + 1)],
                          waits=[fr["b_ln1"], ln1_store[s2]] if k == 0 else (), sem="Lln")
                    toks.append(p.dma("sync", lnl[:, 512 * k:512 * (k + 1)],
                                      ln1l_d[128 * k:128 * (k + 1), SUP * s2:SUP * (s2 + 1)],
                                      sem="Lln"))
                return toks

            ln1_next = emit_ln1_loads(0)

            def emit_wq_pair(n2, extra=()):
                wq16 = b_wq[n2 % 2].bitcast(mybir.dt.bfloat16)
                p.dma("sync", wq16[:, 0:1024].rearrange("p (k n) -> p k n", n=128),
                      wqh_d[:, 128 * n2:128 * (n2 + 1)].rearrange("(k p) n -> p k n", p=128),
                      waits=[fr[f"b_wq{n2 % 2}"]] + list(extra), sem=f"Lwq{n2 % 2}")
                return p.dma("sync", wq16[:, 1024:2048].rearrange("p (k n) -> p k n", n=128),
                             wql_d[:, 128 * n2:128 * (n2 + 1)].rearrange("(k p) n -> p k n", p=128),
                             sem=f"Lwq{n2 % 2}")

            wq_pf = {0: emit_wq_pair(0, (tok_wsplit, tok_splitB_lo)),
                     1: emit_wq_pair(1, (tok_wsplit, tok_splitB_lo))}

            for s in range(nsup):
                ts_tok, tc_tok, tok_sgn2 = rope_next
                ln1_toks = ln1_next
                xt_toks = []
                for k in range(NK):
                    xt_toks.append(p.dma("sync", b_xT[:, 512 * k:512 * (k + 1)],
                                         xT_d[128 * k:128 * (k + 1), SUP * s:SUP * (s + 1)],
                                         waits=[fr["b_xT"], fr["b_he"]] if k == 0 else (),
                                         sem="LxT"))
                qf_toks = []
                q_mm = None
                for n in range(NK):
                    wq16 = b_wq[n % 2].bitcast(mybir.dt.bfloat16)
                    if n < 2:
                        wqt = wq_pf[n]
                    else:
                        wqt = emit_wq_pair(n)
                    for k in range(NK):
                        for ps_ in range(3):
                            lo_, ro_ = [(0, lnh), (0, lnl), (1024, lnh)][ps_]
                            q_mm = p.emit("tensor", lambda e, n=n, k=k, lo_=lo_, ro_=ro_,
                                          wq16=wq16,
                                          st_=(k == 0 and ps_ == 0),
                                          sp=(k == NK - 1 and ps_ == 2): e.matmul(
                                pb[n % 2][:], wq16[:, lo_ + 128 * k:lo_ + 128 * (k + 1)],
                                ro_[:, 512 * k:512 * (k + 1)], start=st_, stop=sp),
                                waits=([wqt, ln1_toks[k], pfree[n % 2]] if (k == 0 and ps_ == 0)
                                       else ([ln1_toks[k]] if ps_ == 0 else [])),
                                inc="T" if (k == NK - 1 and ps_ == 2) else None)
                    fr[f"b_wq{n % 2}"] = q_mm
                    if s == 0 and n == 0 and dbg_wq is not None:
                        fr[f"b_wq{n % 2}"] = p.dma("gpsimd", dbg_wq[:, 0:1024], b_wq[0],
                                                   waits=[q_mm], sem="dst")
                    if s == 0 and n < 2 and dbg_qraw is not None:
                        tqr = p.emit("vector", lambda e, n=n: e.tensor_copy(
                            rs3[:], pb[n % 2][:]), waits=[q_mm], inc="V")
                        fr["qraw_dbg"] = p.dma("gpsimd", dbg_qraw[:, 512 * n:512 * (n + 1)],
                                               rs3[:], waits=[tqr], sem="dst")
                    p.emit("vector", lambda e, n=n: e.tensor_tensor(rs1[:], pb[n % 2][:], cosrep[:],
                                                                    op=OP.mult),
                           waits=[q_mm, tc_tok, fr.get("qraw_dbg")]
                                 + ([fr.get("sqmm")] if n == 0 else []))
                    for off in (0, 32, 64, 96):
                        sw = off ^ 32
                        p.emit("vector", lambda e, n=n, off=off, sw=sw: e.tensor_tensor(
                            rs3[off:off + 32, :], pb[n % 2][sw:sw + 32, :], sinB[off:off + 32, :],
                            op=OP.mult),
                            waits=[tok_sgn2, tc_tok, fr.get("qraw_dbg")] if off == 0 else [])
                    ev = p.emit("vector", lambda e: e.tensor_tensor(rs1[:], rs1[:], rs3[:], op=OP.add),
                                inc="V")
                    pfree[n % 2] = ev
                    tok_mn = p.emit("vector", lambda e: e.tensor_scalar(
                        rs3[:], rs1[:], 0.125, 0.0, op0=OP.mult, op1=OP.min), inc="V")
                    tok_ex = p.emit("scalar", lambda e: e.activation(rs3[:], rs3[:], AF.Exp),
                                    waits=[tok_mn], inc="S")
                    p.emit("vector", lambda e: e.tensor_scalar(rs1[:], rs1[:], 0.125, 0.0,
                                                               op0=OP.mult, op1=OP.max))
                    qf_toks.append(p.emit("vector", lambda e, n=n: e.tensor_tensor(
                        b_qf[:, 512 * n:512 * (n + 1)], rs1[:], rs3[:], op=OP.add),
                        waits=[tok_ex] + ([fr["b_qf"]] if n == 0 else []), inc="V"))
                fr["b_ln1"] = q_mm
                if s + 1 < nsup:
                    rope_next = emit_rope(s + 1)
                    wq_pf = {0: emit_wq_pair(0), 1: emit_wq_pair(1)}
                if dbg_qf is not None:
                    for k in range(NK):
                        fr["qf_dbg"] = p.dma("gpsimd", dbg_qf[128 * k:128 * (k + 1), SUP * s:SUP * (s + 1)],
                              b_qf[:, 512 * k:512 * (k + 1)], waits=[qf_toks[-1]], sem="dst")

                den_mm = None
                for k in range(NK):
                    den_mm = p.emit("tensor", lambda e, k=k: e.matmul(
                        pb[4][0:16, :], bd[:, 16 * k:16 * k + 16], b_qf[:, 512 * k:512 * (k + 1)],
                        start=(k == 0), stop=(k == NK - 1)),
                        waits=[qf_toks[k], tok_bd] + ([pfree[4]] if k == 0 else []),
                        inc="T" if k == NK - 1 else None)
                p.emit("vector", lambda e: e.tensor_scalar(rs1[0:16, :], pb[4][0:16, :], 1e-6, None,
                                                           op0=OP.max), waits=[den_mm])
                tok_den = p.emit("vector", lambda e: e.reciprocal(
                    rowp[0:16, C_ID:C_ID + 512], rs1[0:16, :]), waits=[fr["invden"]], inc="V")
                pfree[4] = tok_den
                if dbg_den is not None:
                    fr["invden"] = p.dma("gpsimd", dbg_den[:, SUP * s:SUP * (s + 1)], rowp[0:16, C_ID:C_ID + 512],
                          waits=[tok_den], sem="dst")

                att_toks = []
                inv_last = None
                for pr in range(8):
                    ib = 2 + (pr % 2)
                    inv_tok = p.emit("tensor", lambda e, pr=pr, ib=ib: e.matmul(
                        pb[ib][:], psel16[0:16, 128 * pr:128 * (pr + 1)],
                        rowp[0:16, C_ID:C_ID + 512], start=True, stop=True),
                        waits=[tok_den, tok_psel, pfree[ib]], inc="T")
                    inv_last = inv_tok
                    mb = pr % 2
                    p.emit("tensor", lambda e, pr=pr, mb=mb: e.matmul(
                        pb[mb][0:64, :], kv_sb[0:64, 130 * pr:130 * pr + 64],
                        b_qf[0:64, 512 * pr:512 * (pr + 1)], start=True, stop=True),
                        waits=[kv_evac[pr // 3], qf_toks[pr], pfree[mb]])
                    att_mm = p.emit("tensor", lambda e, pr=pr, mb=mb: e.matmul(
                        pb[mb][64:128, :], kv_sb[64:128, 130 * pr + 64:130 * pr + 128],
                        b_qf[64:128, 512 * pr:512 * (pr + 1)], start=True, stop=True), inc="T")
                    p.emit("vector", lambda e, ib=ib: e.tensor_copy(rs2[:], pb[ib][:]),
                           waits=[inv_tok])
                    t_att = p.emit("vector", lambda e, pr=pr, mb=mb: e.tensor_tensor(
                        rs1[:], pb[mb][:], rs2[:], op=OP.mult),
                        waits=[att_mm] + ([fr["b_att"]] if pr == 0 else []), inc="V")
                    t_h = p.emit("vector", lambda e, pr=pr: e.tensor_copy(
                        atth[:, 512 * pr:512 * (pr + 1)], rs1[:]), waits=[t_att], inc="V")
                    etok = p.emit("vector", lambda e, pr=pr: e.tensor_tensor(
                        attl[:, 512 * pr:512 * (pr + 1)], rs1[:], atth[:, 512 * pr:512 * (pr + 1)],
                        op=OP.subtract), waits=[t_h], inc="V")
                    att_toks.append(etok)
                    pfree[mb] = etok
                    pfree[ib] = etok
                fr["b_qf"] = att_toks[-1]
                fr["invden"] = inv_last
                if dbg_att is not None:
                    for k in range(NK):
                        fr["att_dbg"] = p.dma("gpsimd", dbg_att[128 * k:128 * (k + 1), SUP * s:SUP * (s + 1)],
                              b_att[:, 512 * k:512 * (k + 1)], waits=[att_toks[-1]], sem="dst")

                y_toks = []
                wo_mm = None
                for n in range(NK):
                    wo16 = b_wo[n % 2].bitcast(mybir.dt.bfloat16)
                    p.dma("sync", wo16[:, 0:1024].rearrange("p (k n) -> p k n", n=128),
                          wouth_d[:, 128 * n:128 * (n + 1)].rearrange("(k p) n -> p k n", p=128),
                          waits=[fr[f"b_wo{n % 2}"]], sem=f"Lwo{n % 2}")
                    wot = p.dma("sync", wo16[:, 1024:2048].rearrange("p (k n) -> p k n", n=128),
                                woutl_d[:, 128 * n:128 * (n + 1)].rearrange("(k p) n -> p k n", p=128),
                                sem=f"Lwo{n % 2}")
                    for k in range(NK):
                        for ps_ in range(3):
                            lo_, ro_ = [(0, atth), (0, attl), (1024, atth)][ps_]
                            wo_mm = p.emit("tensor", lambda e, n=n, k=k, lo_=lo_, ro_=ro_,
                                           wo16=wo16,
                                           st_=(k == 0 and ps_ == 0),
                                           sp=(k == NK - 1 and ps_ == 2): e.matmul(
                                pb[n % 2][:], wo16[:, lo_ + 128 * k:lo_ + 128 * (k + 1)],
                                ro_[:, 512 * k:512 * (k + 1)], start=st_, stop=sp),
                                waits=([wot, att_toks[k], pfree[n % 2]] if (k == 0 and ps_ == 0)
                                       else ([att_toks[k]] if ps_ == 0 else [])),
                                inc="T" if (k == NK - 1 and ps_ == 2) else None)
                    fr[f"b_wo{n % 2}"] = wo_mm
                    ytok = p.emit("vector", lambda e, n=n: e.tensor_tensor(
                        b_yT[:, 512 * n:512 * (n + 1)], pb[n % 2][:], b_xT[:, 512 * n:512 * (n + 1)],
                        op=OP.add),
                        waits=[wo_mm, xt_toks[n]] + ([fr["b_yT"]] if n == 0 else []), inc="V")
                    y_toks.append(ytok)
                    pfree[n % 2] = ytok
                fr["b_att"] = wo_mm
                fr["b_xT"] = y_toks[-1]
                if dbg_y is not None:
                    for k in range(NK):
                        fr["y_dbg"] = p.dma("gpsimd", dbg_y[128 * k:128 * (k + 1), SUP * s:SUP * (s + 1)],
                              b_yT[:, 512 * k:512 * (k + 1)], waits=[y_toks[-1]], sem="dst")
                    fr["b_yT"] = fr["y_dbg"]

                chunks = [(b_yT[:, 512 * k:512 * (k + 1)], y_toks[k]) for k in range(NK)]
                tok_a, tok_b = ln_fm(chunks, 4, 2, 3)
                ln2_toks = []
                for k in range(NK):
                    p.emit("vector", lambda e, k=k: e.tensor_tensor(
                        rs3[:], b_yT[:, 512 * k:512 * (k + 1)], abc[:], op=OP.mult),
                        waits=[tok_a, fr["b_ln1"]] if k == 0 else [])
                    t_ln = p.emit("vector", lambda e, k=k: e.tensor_tensor(rs3[:], rs3[:], bbc[:], op=OP.add),
                                  waits=[tok_b] if k == 0 else [], inc="V")
                    t_h = p.emit("vector", lambda e, k=k: e.tensor_copy(
                        lnh[:, 512 * k:512 * (k + 1)], rs3[:]), waits=[t_ln], inc="V")
                    ln2_toks.append(p.emit("vector", lambda e, k=k: e.tensor_tensor(
                        lnl[:, 512 * k:512 * (k + 1)], rs3[:], lnh[:, 512 * k:512 * (k + 1)],
                        op=OP.subtract), waits=[t_h], inc="V"))
                if dbg_ln2 is not None:
                    for k in range(NK):
                        tdl = p.emit("vector", lambda e, k=k: e.tensor_tensor(
                            rs3[:], lnh[:, 512 * k:512 * (k + 1)], lnl[:, 512 * k:512 * (k + 1)],
                            op=OP.add), waits=[ln2_toks[-1], fr.get("ln2_dbg")], inc="V")
                        fr["ln2_dbg"] = p.dma("gpsimd", dbg_ln2[128 * k:128 * (k + 1), SUP * s:SUP * (s + 1)],
                              rs3[:], waits=[tdl], sem="dst")

                tok_cont = None
                for g in range(2):
                    enc_last = [None] * 8
                    for k in range(NK):
                        e16 = b_enc[k % 2].bitcast(mybir.dt.bfloat16)
                        p.dma("sync", e16[:, 0:2048], ench_d[128 * k:128 * (k + 1), :],
                              waits=[fr[f"b_enc{k % 2}"]] + ([tok_wsplit, tok_splitB_lo]
                                                              if (s == 0 and g == 0 and k < 2) else []),
                              sem=f"Len{k % 2}")
                        et = p.dma("sync", e16[:, 2048:4096], encl_d[128 * k:128 * (k + 1), :],
                                   sem=f"Len{k % 2}")
                        for s8 in range(8):
                            sg = 8 * g + s8
                            for ps_ in range(3):
                                wof, ro_ = [(0, lnh), (0, lnl), (2048, lnh)][ps_]
                                enc_last[s8] = p.emit("tensor", lambda e, k=k, s8=s8, sg=sg,
                                                      wof=wof, ro_=ro_, e16=e16,
                                                      st_=(k == 0 and ps_ == 0),
                                                      sp=(k == NK - 1 and ps_ == 2): e.matmul(
                                    pb[s8][:], e16[:, wof + 128 * sg:wof + 128 * (sg + 1)],
                                    ro_[:, 512 * k:512 * (k + 1)], start=st_, stop=sp),
                                    waits=([et, ln2_toks[k], pfree[s8]] if (k == 0 and ps_ == 0)
                                           else ([et, ln2_toks[k]] if ps_ == 0 else [])),
                                    inc="T" if (s8 == 7 or (k == NK - 1 and ps_ == 2)) else None)
                        fr[f"b_enc{k % 2}"] = enc_last[7]
                    if s == 0 and g == 0 and dbg_encp is not None:
                        tdbg = p.emit("vector", lambda e: e.tensor_copy(rs3[:], pb[0][:]),
                                      waits=[enc_last[0]], inc="V")
                        p.dma("gpsimd", dbg_encp, rs3[:], waits=[tdbg], sem="dst")
                        p.dma("gpsimd", dbg_ln2w, b_ln2[:, 0:512], waits=[enc_last[0]], sem="dst")
                        p.dma("gpsimd", dbg_encw, b_enc[1][:], waits=[enc_last[7]], sem="dst")
                    spk_toks = []
                    for s8 in range(8):
                        sg = 8 * g + s8
                        spk_toks.append(p.emit("scalar", lambda e, s8=s8, sg=sg: e.activation(
                            b_spk[:, 512 * s8:512 * (s8 + 1)], pb[s8][:], AF.Sigmoid,
                            bias=encb[:, sg:sg + 1], scale=1.0),
                            waits=[enc_last[s8]] + ([fr["b_spk"], tok_wsplit, tok_splitB_lo]
                                                    if s8 == 0 else []), inc="S"))
                    if s == 0 and dbg_spk is not None:
                        for s8 in range(8):
                            p.dma("gpsimd", dbg_spk[:, 512 * (8 * g + s8) - 4096 * g if False else 512 * s8 + 0:512 * s8 + 512] if False else dbg_spk[:, 2048 * g + 256 * s8:2048 * g + 256 * s8 + 256],
                                  b_spk[:, 512 * s8:512 * s8 + 256], waits=[spk_toks[s8]], sem="dst")
                    s2c_mm = None
                    for s8 in range(8):
                        sg = 8 * g + s8
                        s2c_mm = p.emit("tensor", lambda e, s8=s8, sg=sg: e.matmul(
                            pb[0][0:64, :], s2cw[:, 64 * sg:64 * (sg + 1)],
                            b_spk[:, 512 * s8:512 * (s8 + 1)], start=(s8 == 0), stop=(s8 == 7)),
                            waits=[spk_toks[s8]] + ([spk_toks[0], tok_wsmall] if s8 == 0 else []),
                            inc="T" if s8 == 7 else None)
                    if g == 0:
                        tok_cont = p.emit("vector", lambda e: e.tensor_copy(contT[:], pb[0][0:64, :]),
                                          waits=[s2c_mm, fr["contT"]], inc="V")
                    else:
                        p.emit("vector", lambda e: e.tensor_tensor(contT[:], contT[:], pb[0][0:64, :],
                                                                   op=OP.add), waits=[s2c_mm])
                        tok_cont = p.emit("vector", lambda e: e.tensor_scalar(
                            contT[:], contT[:], s2cb[:], None, op0=OP.add), inc="V")
                    for i in range(8):
                        pfree[i] = tok_cont
                fr["b_ln1"] = fr.get("ln2_dbg") or enc_last[7]
                if s + 1 < nsup:
                    ln1_next = emit_ln1_loads(s + 1)
                if dbg_cont is not None:
                    fr["contT"] = p.dma("gpsimd", dbg_cont[:, SUP * s:SUP * (s + 1)], contT[:], waits=[tok_cont],
                          sem="dst")

                r1 = p.emit("tensor", lambda e: e.matmul(pb[4][0:64, :], rw1[:], contT[:],
                                                         start=True, stop=True),
                            waits=[tok_cont, tok_wsmall, pfree[4]], inc="T")
                rt = p.emit("scalar", lambda e: e.activation(rowp[64:128, C_RT:C_RT + 512], pb[4][0:64, :],
                                                             AF.Tanh, bias=rb1t[64:128, :], scale=1.0),
                            waits=[r1, fr["tanh"], fr.get("tanh_dbg")], inc="S")
                r2 = p.emit("tensor", lambda e: e.matmul(pb[4][0:8, :], rw2t[64:128, :],
                                                         rowp[64:128, C_RT:C_RT + 512],
                                                         start=True, stop=True), waits=[rt], inc="T")
                fr["tanh"] = r2
                rl = p.emit("vector", lambda e: e.tensor_scalar(rowp[32:40, C_RT:C_RT + 512],
                                                                pb[4][0:8, :], rb2[:], None, op0=OP.add),
                            waits=[r2, fr["logit"], fr.get("lgf_dbg")], inc="V")
                if s == 0 and dbg_tanh is not None:
                    fr["tanh_dbg"] = p.dma("gpsimd", dbg_tanh, rowp[64:128, C_RT:C_RT + 512],
                                           waits=[rl], sem="dst")
                    fr["lgf_dbg"] = p.dma("gpsimd", dbg_lgf, rowp[32:40, C_RT:C_RT + 512],
                                          waits=[rl], sem="dst")
                pfree[4] = rl
                tr_last = None
                for u in range(4):
                    tr_last = p.emit("tensor", lambda e, u=u: e.transpose(
                        pb[4][:, 8 * u:8 * u + 8], rowp[32:40, C_RT + 128 * u:C_RT + 128 * (u + 1)],
                        ident[32:40, 32:40]),
                        waits=[rl, tok_ident] + ([pfree[4]] if u == 0 else []), inc="T")
                fr["logit"] = tr_last
                tg = p.emit("vector", lambda e: e.tensor_copy(gtm[:], pb[4][:, 0:32]),
                            waits=[tr_last, fr.get("gtm2_dbg")], inc="V")
                pfree[4] = tg
                if s == 0 and dbg_lgtm is not None:
                    fr["lgtm_dbg"] = p.dma("gpsimd", dbg_lgtm, gtm[:], waits=[tg], sem="dst")
                g3 = gtm.ap().rearrange("p (u v) -> p u v", v=8)
                tok_ex = p.emit("scalar", lambda e: e.activation(rs2[:, 0:32], gtm[:], AF.Exp),
                                waits=[tg, fr.get("rs2V"), fr.get("lgtm_dbg")], inc="S")
                for u in range(4):
                    p.emit("vector", lambda e, u=u: e.max(rs1[:, 8 * u:8 * (u + 1)],
                                                          rs2[:, 8 * u:8 * (u + 1)]),
                           waits=[tok_ex] if u == 0 else [])
                for u in range(4):
                    p.emit("vector", lambda e, u=u: e.tensor_scalar(
                        gtm[:, 8 * u:8 * (u + 1)], rs2[:, 8 * u:8 * (u + 1)],
                        rs1[:, 8 * u + 1:8 * u + 2], None, op0=OP.is_ge))
                p.emit("vector", lambda e: e.tensor_tensor(rs2[:, 0:32], rs2[:, 0:32], gtm[:],
                                                           op=OP.mult))
                for u in range(4):
                    p.emit("vector", lambda e, u=u: e.tensor_tensor(
                        rs1[:, 8 * u + 2:8 * u + 3], rs1[:, 8 * u:8 * u + 1],
                        rs1[:, 8 * u + 1:8 * u + 2], op=OP.add))
                for u in range(4):
                    p.emit("vector", lambda e, u=u: e.reciprocal(rs1[:, 8 * u + 3:8 * u + 4],
                                                                 rs1[:, 8 * u + 2:8 * u + 3]))
                for u in range(4):
                    tok_gt = p.emit("vector", lambda e, u=u: e.tensor_scalar(
                        gtm[:, 8 * u:8 * (u + 1)], rs2[:, 8 * u:8 * (u + 1)],
                        rs1[:, 8 * u + 3:8 * u + 4], None, op0=OP.mult),
                        inc="V" if u == 3 else None)
                if s == 0 and dbg_gtm2 is not None:
                    fr["gtm2_dbg"] = p.dma("gpsimd", dbg_gtm2, gtm[:], waits=[tok_gt], sem="dst")
                gt_last = None
                for u in range(4):
                    gt_last = p.emit("tensor", lambda e, u=u: e.transpose(
                        pb[4][0:8, 128 * u:128 * (u + 1)], gtm[:, 8 * u:8 * u + 8], ident[:]),
                        waits=[tok_gt, fr.get("gtm2_dbg")] + ([pfree[4]] if u == 0 else []), inc="T")
                tok_gatesT = p.emit("vector", lambda e: e.tensor_copy(
                    rowp[0:8, C_RT:C_RT + 512], pb[4][0:8, :]),
                    waits=[gt_last, fr["gates"]], inc="V")
                pfree[4] = tok_gatesT
                if dbg_gat is not None:
                    fr["gates"] = p.dma("gpsimd", dbg_gat[:, SUP * s:SUP * (s + 1)], rowp[0:8, C_RT:C_RT + 512],
                          waits=[tok_gatesT], sem="dst")

                bt = p.emit("tensor", lambda e: e.matmul(pb[4][0:64, :], be2[:],
                                                         rowp[0:8, C_RT:C_RT + 512],
                                                         start=True, stop=True),
                            waits=[tok_gatesT, pfree[4]], inc="T")
                tok_acc = p.emit("vector", lambda e: e.tensor_copy(accT[:], pb[4][0:64, :]),
                                 waits=[bt, fr["accT"]], inc="V")
                pfree[4] = tok_acc
                gates_last = bt
                for e_ in range(8):
                    eb = e_ % 2
                    w1t = p.dma("gpsimd", b_we1[eb], we1_d[e_], waits=[fr[f"b_we1{eb}"]], sem=f"Lw1{eb}")
                    w2t = p.dma("gpsimd", b_we2[eb].rearrange("p (c m) -> p c m", m=64),
                                we2_d[e_].rearrange("(c p) m -> p c m", p=128),
                                waits=[fr[f"b_we2{eb}"]], sem=f"Lw2{eb}")
                    he_toks = []
                    for hk in range(NK):
                        hb = hk % 2
                        hm = p.emit("tensor", lambda e, hk=hk, eb=eb, hb=hb: e.matmul(
                            pb[hb][:], b_we1[eb][:, 128 * hk:128 * (hk + 1)], contT[:],
                            start=True, stop=True),
                            waits=[w1t, tok_cont, pfree[hb]] +
                                  ([fr["b_he"]] if (e_ == 0 and hk == 0) else []),
                            inc="T")
                        het = p.emit("scalar", lambda e, e_=e_, hk=hk, hb=hb: e.activation(
                            b_he[:, 512 * hk:512 * (hk + 1)], pb[hb][:], AF.Relu,
                            bias=be1c[:, 8 * e_ + hk:8 * e_ + hk + 1], scale=1.0),
                            waits=[hm], inc="S")
                        he_toks.append(het)
                        pfree[hb] = het
                    fr[f"b_we1{eb}"] = he_toks[-1]
                    rlo = 64 * (e_ % 2)
                    mm2 = None
                    for hk in range(NK):
                        mm2 = p.emit("tensor", lambda e, hk=hk, eb=eb, rlo=rlo: e.matmul(
                            pb[6][rlo:rlo + 64, :], b_we2[eb][:, 64 * hk:64 * (hk + 1)],
                            b_he[:, 512 * hk:512 * (hk + 1)], start=(hk == 0), stop=(hk == NK - 1)),
                            waits=[w2t, he_toks[hk]] +
                                  ([pfree[6]] if (hk == 0 and e_ % 2 == 0) else []),
                            inc="T" if hk == NK - 1 else None)
                    fr[f"b_we2{eb}"] = mm2
                    fr["b_he"] = mm2
                    if e_ % 2 == 1:
                        gb = p.emit("tensor", lambda e, e_=e_: e.matmul(
                            pb[7][:], psel16[0:8, 128 * (e_ // 2):128 * (e_ // 2 + 1)],
                            rowp[0:8, C_RT:C_RT + 512],
                            start=True, stop=True), waits=[tok_gatesT, pfree[7]], inc="T")
                        gates_last = gb
                        p.emit("vector", lambda e: e.tensor_copy(rs2[:], pb[7][:]),
                               waits=[gb])
                        gm = p.emit("vector", lambda e: e.tensor_tensor(rs1[:], pb[6][:], rs2[:],
                                                                        op=OP.mult),
                                    waits=[mm2, tok_acc], inc="V")
                        pfree[7] = gm
                        pfree[6] = gm
                        p.emit("vector", lambda e: e.tensor_tensor(accT[:], accT[:], rs1[0:64, :],
                                                                   op=OP.add))
                        p.emit("vector", lambda e: e.tensor_copy(rs3[0:64, :], rs1[64:128, :]))
                        tok_acc = p.emit("vector", lambda e: e.tensor_tensor(
                            accT[:], accT[:], rs3[0:64, :], op=OP.add), inc="V")
                fr["gates"] = gates_last
                fr["contT"] = mm2
                if dbg_acc is not None:
                    fr["accT"] = p.dma("gpsimd", dbg_acc[:, SUP * s:SUP * (s + 1)], accT[:], waits=[tok_acc],
                          sem="dst")

                dec_toks = []
                acc_last = None
                moe_toks = {}
                b_moeX = [b_moe, b_moe1]
                for h2 in range(2):
                    for sg in range(16):
                        cb = sg % 2
                        cm = p.emit("tensor", lambda e, sg=sg, h2=h2, cb=cb: e.matmul(
                            pb[cb][:, 0:256], b_c2s[:, 128 * sg:128 * (sg + 1)],
                            accT[0:64, 256 * h2:256 * (h2 + 1)], start=True, stop=True),
                            waits=[tok_acc, tok_c2sw, pfree[cb]], inc="T")
                        acc_last = cm
                        mt = p.emit("scalar", lambda e, sg=sg, cb=cb, h2=h2: e.activation(
                            b_moeX[h2][:, 256 * sg:256 * (sg + 1)], pb[cb][:, 0:256], AF.Sigmoid,
                            bias=c2sb[:, sg:sg + 1], scale=1.0),
                            waits=[cm] + ([fr["b_spk"]] if (h2 == 0 and sg == 0) else [])
                                 + ([fr["b_att"], fr.get("att_dbg")] if (h2 == 1 and sg == 0) else []),
                            inc="S")
                        moe_toks[(h2, sg)] = mt
                        pfree[cb] = mt
                dec_mm = [None] * 8
                for sg in range(16):
                    db = sg % 2
                    dt_ = p.dma("gpsimd", b_dec[db], decw_d[128 * sg:128 * (sg + 1), :],
                                waits=[fr[f"b_dec{db}"]], sem=f"Ldc{db}")
                    for dk in range(8):
                        for h2 in range(2):
                            dec_mm[dk] = p.emit("tensor", lambda e, sg=sg, dk=dk, db=db, h2=h2: e.matmul(
                                pb[dk][:, 256 * h2:256 * (h2 + 1)], b_dec[db][:, 128 * dk:128 * (dk + 1)],
                                b_moeX[h2][:, 256 * sg:256 * (sg + 1)],
                                start=(sg == 0 and h2 == 0), stop=(sg == 15 and h2 == 1)),
                                waits=([dt_, moe_toks[(0, sg)], moe_toks[(1, sg)]] if (dk == 0 and h2 == 0) else [])
                                      + ([pfree[dk]] if (sg == 0 and h2 == 0) else []),
                                inc="T" if (dk == 7 or (sg == 15 and h2 == 1)) else None)
                    fr[f"b_dec{db}"] = dec_mm[7]
                for dk in range(8):
                    dct = p.emit("scalar", lambda e, dk=dk: e.activation(
                        b_decT[:, 512 * dk:512 * (dk + 1)],
                        pb[dk][:], AF.Sigmoid, bias=decb[:, dk:dk + 1], scale=1.0),
                        waits=[dec_mm[dk]] + ([fr["b_qf"], fr.get("qf_dbg"), fr.get("lnb_dbg")]
                                              if dk == 0 else []),
                        inc="S")
                    dec_toks.append(dct)
                    pfree[dk] = dct
                fr["b_spk"] = dec_mm[7]
                fr["b_moe1"] = dec_mm[7]
                fr["accT"] = acc_last
                tok_dec = dec_toks[-1]
                if dbg_dec is not None:
                    for k in range(NK):
                        fr["dec_dbg"] = p.dma("gpsimd", dbg_dec[128 * k:128 * (k + 1), SUP * s:SUP * (s + 1)],
                              b_decT[:, 512 * k:512 * (k + 1)], waits=[tok_dec], sem="dst")

                chunks = [(b_decT[:, 512 * k:512 * (k + 1)], tok_dec) for k in range(NK)]
                tok_a, tok_b = ln_fm(chunks, 4, 2, 3)
                o_toks = []
                for k in range(NK):
                    p.emit("vector", lambda e, k=k: e.tensor_tensor(
                        rs3[:], b_decT[:, 512 * k:512 * (k + 1)], abc[:], op=OP.mult),
                        waits=[tok_a, fr["b_att"]] if k == 0 else [])
                    p.emit("vector", lambda e, k=k: e.tensor_tensor(rs3[:], rs3[:], bbc[:], op=OP.add),
                           waits=[tok_b] if k == 0 else [])
                    p.emit("vector", lambda e, k=k: e.tensor_scalar(
                        rs3[:], rs3[:], lng[:, k:k + 1], lnb[:, k:k + 1], op0=OP.mult, op1=OP.add))
                    o_toks.append(p.emit("vector", lambda e, k=k: e.tensor_tensor(
                        b_outT[:, 512 * k:512 * (k + 1)], rs3[:], b_yT[:, 512 * k:512 * (k + 1)],
                        op=OP.add), waits=[y_toks[k], fr.get("att_dbg")], inc="V"))
                st_last = None
                for k in range(NK):
                    st_last = p.dma("gpsimd", outT_d[128 * k:128 * (k + 1), SUP * s:SUP * (s + 1)],
                                    b_outT[:, 512 * k:512 * (k + 1)], waits=[o_toks[k]], sem="dst")
                    out_store_toks.append(st_last)
                fr["b_att"] = st_last
                fr["b_yT"] = o_toks[-1]
                fr["b_qf"] = fr.get("dec_dbg") or o_toks[-1]
                fr["rs2V"] = o_toks[-1]

        p.emit("gpsimd", None, waits=[("dst", p.emitted["dst"], None)])
        p.simulate()
        p.replay(block)

    return nc, dbg_t



_NC_CACHE = {}


def _get_nc():
    if "nc" not in _NC_CACHE:
        _NC_CACHE["nc"] = build_core(debug=(), phases=2)[0]
    return _NC_CACHE["nc"]


def kernel(L0_attn_x, L0_attn_W_qkv, L0_attn_W_out, L0_ffn_enc_W, L0_ffn_enc_b,
           L0_ffn_s2c_W, L0_ffn_s2c_b, L0_ffn_rW1, L0_ffn_rb1, L0_ffn_rW2, L0_ffn_rb2,
           L0_ffn_We1, L0_ffn_be1, L0_ffn_We2, L0_ffn_be2, L0_ffn_c2s_W, L0_ffn_c2s_b,
           L0_ffn_dec_W, L0_ffn_dec_b, L0_ffn_ln_g, L0_ffn_ln_b):
    x = np.asarray(L0_attn_x, dtype=np.float32)
    weights = {
        "wqkv": np.ascontiguousarray(np.asarray(L0_attn_W_qkv, np.float32)),
        "wout": np.ascontiguousarray(np.asarray(L0_attn_W_out, np.float32)),
        "encw": np.ascontiguousarray(np.asarray(L0_ffn_enc_W, np.float32)),
        "encb": np.ascontiguousarray(np.asarray(L0_ffn_enc_b, np.float32)),
        "s2cw": np.ascontiguousarray(np.asarray(L0_ffn_s2c_W, np.float32)),
        "s2cb": np.ascontiguousarray(np.asarray(L0_ffn_s2c_b, np.float32)),
        "rw1": np.ascontiguousarray(np.asarray(L0_ffn_rW1, np.float32)),
        "rb1": np.ascontiguousarray(np.asarray(L0_ffn_rb1, np.float32)),
        "rw2": np.ascontiguousarray(np.asarray(L0_ffn_rW2, np.float32)),
        "rb2": np.ascontiguousarray(np.asarray(L0_ffn_rb2, np.float32)),
        "we1": np.ascontiguousarray(np.asarray(L0_ffn_We1, np.float32)),
        "be1": np.ascontiguousarray(np.asarray(L0_ffn_be1, np.float32)),
        "we2": np.ascontiguousarray(np.asarray(L0_ffn_We2, np.float32)),
        "be2": np.ascontiguousarray(np.asarray(L0_ffn_be2, np.float32)),
        "c2sw": np.ascontiguousarray(np.asarray(L0_ffn_c2s_W, np.float32)),
        "c2sb": np.ascontiguousarray(np.asarray(L0_ffn_c2s_b, np.float32)),
        "decw": np.ascontiguousarray(np.asarray(L0_ffn_dec_W, np.float32)),
        "decb": np.ascontiguousarray(np.asarray(L0_ffn_dec_b, np.float32)),
        "lng": np.ascontiguousarray(np.asarray(L0_ffn_ln_g, np.float32)),
        "lnb": np.ascontiguousarray(np.asarray(L0_ffn_ln_b, np.float32)),
    }
    n_cores = 8
    in_maps = []
    for c in range(n_cores):
        im = dict(weights)
        im["xT"] = np.ascontiguousarray(x[c].T)
        in_maps.append(im)
    nc = _get_nc()
    res = run_bass_kernel_spmd(nc, in_maps, core_ids=list(range(n_cores)))
    out = np.empty((n_cores, T, D), dtype=np.float32)
    for c in range(n_cores):
        out[c] = np.asarray(res.results[c]["outT"]).T
    return out
